# version 20 (final)
# baseline (speedup 1.0000x reference)
import contextlib
import numpy as np
import concourse.bass as bass
import concourse.mybir as mybir
from concourse.bass_utils import run_bass_kernel_spmd

F32 = mybir.dt.float32
BF16 = mybir.dt.bfloat16
AF = mybir.ActivationFunctionType
ALU = mybir.AluOpType
AX = mybir.AxisListType

P = 128
C = 1024
CG = 8
T_CORE = 2048
NSLOT = 16
HEADS = 16
HS = 64
E = 8
NEG = -1e9
CAP = 768
NTP = CAP // P
BIG = 1.0e6
I32 = mybir.dt.int32


class Op:
    __slots__ = ("idx", "engine", "fn", "kind", "deps", "needs_inc", "sem", "val")

    def __init__(self, idx, engine, fn, kind):
        self.idx = idx
        self.engine = engine
        self.fn = fn
        self.kind = kind
        self.deps = []
        self.needs_inc = False
        self.sem = None
        self.val = 0


class Emitter:
    ENGINES = ("sync", "scalar", "vector", "tensor", "gpsimd")

    def __init__(self, keep_pe_edges=False):
        self.ops = []
        self.last_writer = {}
        self.readers = {}
        self.keep_pe_edges = keep_pe_edges
        self.last_on = {}
        self.last_dma_slot = {}
        self.dma_count = {}
        self.pending_fence = {}

    def fence(self):
        snap = list(self.last_on.values()) + list(self.last_dma_slot.values())
        for e in self.ENGINES:
            self.pending_fence[e] = list(snap)

    def emit(self, engine, fn, reads=(), writes=(), kind="cmp"):
        op = Op(len(self.ops), engine, fn, kind)
        deps = set()
        pf = self.pending_fence.pop(engine, None)
        if pf:
            deps.update(pf)
        for t in reads:
            w = self.last_writer.get(t)
            if w is not None:
                deps.add(w)
            self.readers.setdefault(t, []).append(op)
        for t in writes:
            w = self.last_writer.get(t)
            if w is not None:
                deps.add(w)
            for r in self.readers.get(t, ()):
                deps.add(r)
            self.last_writer[t] = op
            self.readers[t] = []
        deps.discard(op)
        if self.keep_pe_edges:
            op.deps = list(deps)
        else:
            op.deps = [d for d in deps
                       if not (d.engine == "tensor" and engine == "tensor"
                               and d.kind == "cmp" and kind == "cmp")]
        self.ops.append(op)
        self.last_on[engine] = op
        if kind == "dma":
            s = self.dma_count.get(engine, 0)
            self.dma_count[engine] = s + 1
            self.last_dma_slot[(engine, s % self.NDMASLOT)] = op
        return op

    NDMASLOT = 8

    def finalize(self):
        for op in self.ops:
            for d in op.deps:
                d.needs_inc = True
        counters = {}
        slot_rr = {}
        for op in self.ops:
            if op.kind == "dma":
                s = slot_rr.get(op.engine, 0)
                slot_rr[op.engine] = (s + 1) % self.NDMASLOT
                key = (op.engine, "dma", s)
                counters[key] = counters.get(key, 0) + 16
                op.sem = key
                op.val = counters[key]
            elif op.needs_inc:
                key = (op.engine, op.kind)
                counters[key] = counters.get(key, 0) + 1
                op.sem = key
                op.val = counters[key]
        self.sem_keys = sorted(set(op.sem for op in self.ops if op.sem is not None))
        self.final_counts = dict(counters)

    def replay(self, engine_name, eng, sems):
        watermark = {}
        for op in self.ops:
            if op.engine != engine_name:
                continue
            need = {}
            for d in op.deps:
                if d.sem is None:
                    continue
                if d.val > need.get(d.sem, 0):
                    need[d.sem] = d.val
            if op.kind == "dma" and op.val > 16:
                if op.val - 16 > need.get(op.sem, 0):
                    need[op.sem] = op.val - 16
            for k in sorted(need, key=str):
                v = need[k]
                if watermark.get(k, 0) >= v:
                    continue
                eng.wait_ge(sems[k], v)
                watermark[k] = v
            ins = op.fn(eng)
            if op.needs_inc:
                ins.then_inc(sems[op.sem], 16 if op.kind == "dma" else 1)


@contextlib.contextmanager
def _sem_stack(nc, em):
    with contextlib.ExitStack() as st:
        sems = {}
        for key in em.sem_keys:
            sems[key] = st.enter_context(nc.semaphore("s_" + "_".join(map(str, key))))
        yield sems


def build_program(dbg=False, keep_pe_edges=False):
    nc = bass.Bass("TRN2", target_bir_lowering=False, debug=False)

    x_seq = nc.dram_tensor("x_seq", [4096, C], F32, kind="ExternalInput").ap()
    w_attn = nc.dram_tensor("w_attn", [C, 2 * C], F32, kind="ExternalInput").ap()
    b_attn = nc.dram_tensor("b_attn", [2 * C], F32, kind="ExternalInput").ap()
    head_w = nc.dram_tensor("head_w", [HEADS, C], F32, kind="ExternalInput").ap()
    wr_in = nc.dram_tensor("wr_in", [C, E], F32, kind="ExternalInput").ap()
    w1_in = nc.dram_tensor("w1_in", [E, C, 4096], F32, kind="ExternalInput").ap()
    w2_in = nc.dram_tensor("w2_in", [E, 4096, C], F32, kind="ExternalInput").ap()
    tri01 = nc.dram_tensor("tri01", [P, P], BF16, kind="ExternalInput").ap()
    iota01 = nc.dram_tensor("iota01", [P, P], BF16, kind="ExternalInput").ap()
    padv = nc.dram_tensor("padv", [P, 1], F32, kind="ExternalInput").ap()

    y_out = nc.dram_tensor("y_out", [T_CORE, C], F32, kind="ExternalOutput").ap()
    y_attn = nc.dram_tensor("y_attn_scratch", [T_CORE, C], F32,
                            kind="ExternalOutput" if dbg else "Internal").ap()
    yp = nc.dram_tensor("yp_scratch", [E * CAP, C], BF16, kind="Internal").ap()
    if dbg:
        d_xnown = nc.dram_tensor("d_xnown", [P, NSLOT * C], BF16, kind="ExternalOutput").ap()
        d_xnoth = nc.dram_tensor("d_xnoth", [P, NSLOT * C], BF16, kind="ExternalOutput").ap()
        d_kfm = nc.dram_tensor("d_kfm", [P, 8 * 4096], BF16, kind="ExternalOutput").ap()
        d_probs = nc.dram_tensor("d_probs", [P, 4096], BF16, kind="ExternalOutput").ap()
        d_hwall = nc.dram_tensor("d_hwall", [P, HEADS * C], BF16, kind="ExternalOutput").ap()
        d_qslot = nc.dram_tensor("d_qslot", [P, C], BF16, kind="ExternalOutput").ap()
        d_zsb = nc.dram_tensor("d_zsb", [P, NSLOT * E], F32, kind="ExternalOutput").ap()
        d_wgate = nc.dram_tensor("d_wgate", [P, NSLOT * E], F32, kind="ExternalOutput").ap()
        d_small = nc.dram_tensor("d_small", [P, 8], F32, kind="ExternalOutput").ap()
        d_xnefm = nc.dram_tensor("d_xnefm", [P, CG * T_CORE], BF16, kind="ExternalOutput").ap()
        d_w1bf = nc.dram_tensor("d_w1bf", [P, CG * C], BF16, kind="ExternalOutput").ap()
        d_w2bf = nc.dram_tensor("d_w2bf", [P, CG * C], BF16, kind="ExternalOutput").ap()
        d_hid = nc.dram_tensor("d_hid", [P, 2 * 4096], BF16, kind="ExternalOutput").ap()
        d_yinit = nc.dram_tensor("d_yinit", [P, NSLOT * C], F32, kind="ExternalOutput").ap()
        d_yacc0 = nc.dram_tensor("d_yacc0", [P, NSLOT * C], F32, kind="ExternalOutput").ap()
        d_xnefm0 = nc.dram_tensor("d_xnefm0", [P, CG * T_CORE], BF16, kind="ExternalOutput").ap()
        d_hid00 = nc.dram_tensor("d_hid00", [P, 4096], BF16, kind="ExternalOutput").ap()
        d_w1bf0 = nc.dram_tensor("d_w1bf0", [P, CG * C], BF16, kind="ExternalOutput").ap()
        d_w2bf0 = nc.dram_tensor("d_w2bf0", [P, CG * C], BF16, kind="ExternalOutput").ap()

    em = Emitter(keep_pe_edges=keep_pe_edges)

    def dma(queue, out, in_, reads, writes, **kw):
        em.emit(queue, lambda e, o=out, i=in_, kw=kw: e.dma_start(out=o, in_=i, **kw),
                reads=reads, writes=writes, kind="dma")

    def cmp(engine, fn, reads, writes):
        em.emit(engine, fn, reads=reads, writes=writes, kind="cmp")

    ga = nc.alloc_sbuf_tensor

    xn_own = ga("xn_own", [P, NSLOT * C], BF16).ap()
    b_sb = ga("b_sb", [P, 16], F32).ap()
    wr_bf = ga("wr_bf", [P, 64], BF16).ap()
    w_gate = ga("w_gate", [P, NSLOT * E], F32).ap()
    ident = ga("ident", [P, P], BF16).ap()
    tri_bf = ga("tri_bf", [P, P], BF16).ap()
    padv_sb = ga("padv_sb", [P, 1], F32).ap()
    eps_t = ga("eps_t", [P, 1], F32).ap()
    ones_row = ga("ones_row", [1, P], F32).ap()

    dest_f = ga("dest_f", [P, NSLOT * E], F32).ap()
    iab_i = ga("iab_i", [P, 32], I32).ap()

    cmp("gpsimd", lambda e: e.memset(eps_t[:], 1e-5), [], ["eps"])
    cmp("gpsimd", lambda e: e.memset(ones_row[:], 1.0), [], ["ones"])
    cmp("gpsimd", lambda e: e.memset(ident[:], 0.0), [], ["ident"])
    cmp("gpsimd", lambda e: e.affine_select(
        out=ident[:], in_=ident[:], compare_op=ALU.not_equal, fill=1.0,
        base=0, pattern=[[-1, P]], channel_multiplier=1), ["ident"], ["ident"])
    dma("gpsimd", tri_bf[:], tri01, [], ["tri"])
    dma("gpsimd", padv_sb[:], padv, [], ["padv"])
    dma("gpsimd", b_sb[:], b_attn.rearrange("(g p) -> p g", p=P), [], ["b_sb"],
        allow_slow_non_contiguous=True)

    with contextlib.ExitStack() as _stk:
        xn_oth_h = _stk.enter_context(nc.sbuf_tensor("xn_oth", [P, 16 * C], BF16))
        k_fm_h = _stk.enter_context(nc.sbuf_tensor("k_fm", [P, 8 * 4096], BF16))
        wq_bf_h = _stk.enter_context(nc.sbuf_tensor("wq_bf", [P, 8 * C], BF16))
        xn_oth = xn_oth_h.ap()
        k_fm = k_fm_h.ap()
        wq_bf = wq_bf_h.ap()

        def xn_tile(p):
            if p < 16:
                return xn_own[:, p * C:(p + 1) * C], f"xn{p}"
            return xn_oth[:, (p - 16) * C:(p - 15) * C], f"xn{p}"

        with contextlib.ExitStack() as _stk:
            x_stage_h = _stk.enter_context(nc.sbuf_tensor("x_stage", [P, 4 * C], F32))
            xfb0_h = _stk.enter_context(nc.sbuf_tensor("xfb0", [P, 8 * 512], BF16))
            xfb1_h = _stk.enter_context(nc.sbuf_tensor("xfb1", [P, 8 * 512], BF16))
            wk_bf_h = _stk.enter_context(nc.sbuf_tensor("wk_bf", [P, 8 * C], BF16))
            wst0_h = _stk.enter_context(nc.sbuf_tensor("wst0", [P, C], F32))
            wst1_h = _stk.enter_context(nc.sbuf_tensor("wst1", [P, C], F32))
            wr_st_h = _stk.enter_context(nc.sbuf_tensor("wr_st", [P, 64], F32))
            z_sb_h = _stk.enter_context(nc.sbuf_tensor("z_sb", [P, NSLOT * E], F32))
            psA0_h = _stk.enter_context(nc.psum_tensor("psA0", [P, 1024], BF16))
            psA1_h = _stk.enter_context(nc.psum_tensor("psA1", [P, 1024], BF16))
            psK0_h = _stk.enter_context(nc.psum_tensor("psK0", [P, 512], F32))
            psK1_h = _stk.enter_context(nc.psum_tensor("psK1", [P, 512], F32))
            psZ_h = _stk.enter_context(nc.psum_tensor("psZ", [P, 512], F32))
            x_stage = x_stage_h.ap()
            xfb = [xfb0_h.ap(), xfb1_h.ap()]
            wk_bf = wk_bf_h.ap()
            wst = [wst0_h.ap(), wst1_h.ap()]
            wr_st = wr_st_h.ap()
            z_sb = z_sb_h.ap()
            psA = [psA0_h.ap(), psA1_h.ap()]
            psK = [psK0_h.ap(), psK1_h.ap()]
            psZ = psZ_h.ap()
            stat = {n: [_stk.enter_context(nc.sbuf_tensor(f"st_{n}{r}", [P, 1], F32)).ap()
                        for r in range(2)]
                    for n in ("sumx", "sumsq", "mean", "nmsq", "var", "lnv", "rstd",
                              "m1", "m2", "gs", "grs")}
            g8 = {n: [_stk.enter_context(nc.sbuf_tensor(f"g8_{n}{r}", [P, E], F32)).ap()
                      for r in range(2)]
                  for n in ("eq", "tmp", "zs", "ex", "ind", "es")}

            for half, dest in ((0, wq_bf), (1, wk_bf)):
                for cg in range(CG):
                    r = cg % 2
                    dma("sync", wst[r][:],
                        w_attn[cg * P:(cg + 1) * P, half * C:(half + 1) * C],
                        [f"wst{r}"], [f"wst{r}"])
                    cmp("gpsimd",
                        lambda e, r=r, d=dest, cg=cg: e.tensor_copy(
                            d[:, cg * C:(cg + 1) * C], wst[r][:]),
                        [f"wst{r}"], ["wqk"])
            dma("gpsimd", wr_st[:].rearrange("q (g f) -> q g f", g=CG),
                wr_in.rearrange("(g p) f -> p g f", p=P),
                [], ["wr_st"])
            cmp("gpsimd", lambda e: e.tensor_copy(wr_bf[:], wr_st[:]), ["wr_st"], ["wr"])

            for ib in range(8):
                dma("gpsimd", x_stage[:].rearrange("q (t c) -> q t c", t=4),
                    x_seq[ib * 512:(ib + 1) * 512, :].rearrange("(t p) c -> p t c", p=P),
                    ["xst"], ["xst"])
                for i in range(4):
                    p = ib * 4 + i
                    r = p % 2
                    xt = x_stage[:, i * C:(i + 1) * C]
                    xn_ap, xn_tok = xn_tile(p)
                    cmp("scalar", lambda e, xt=xt, r=r: e.activation(
                        out=wst[r][:], in_=xt, func=AF.Square,
                        accum_out=stat["sumsq"][r][:]),
                        ["xst", f"wst{r}"], [f"wst{r}", f"sumsq{r}"])
                    cmp("vector", lambda e, xt=xt, r=r: e.reduce_sum(
                        stat["sumx"][r][:], xt, axis=AX.X),
                        ["xst"], [f"sumx{r}"])
                    cmp("vector", lambda e, r=r: e.tensor_scalar_mul(
                        stat["mean"][r][:], stat["sumx"][r][:], 1.0 / C),
                        [f"sumx{r}"], [f"mean{r}"])
                    cmp("vector", lambda e, r=r: e.scalar_tensor_tensor(
                        stat["nmsq"][r][:], stat["mean"][r][:], -1.0, stat["mean"][r][:],
                        ALU.mult, ALU.mult),
                        [f"mean{r}"], [f"nmsq{r}"])
                    cmp("vector", lambda e, r=r: e.scalar_tensor_tensor(
                        stat["var"][r][:], stat["sumsq"][r][:], 1.0 / C, stat["nmsq"][r][:],
                        ALU.mult, ALU.add),
                        [f"sumsq{r}", f"nmsq{r}"], [f"var{r}"])
                    cmp("scalar", lambda e, r=r: e.activation(
                        out=stat["lnv"][r][:], in_=stat["var"][r][:], func=AF.Ln,
                        bias=eps_t[:]),
                        [f"var{r}", "eps"], [f"lnv{r}"])
                    cmp("scalar", lambda e, r=r: e.activation(
                        out=stat["rstd"][r][:], in_=stat["lnv"][r][:], func=AF.Exp,
                        scale=-0.5),
                        [f"lnv{r}"], [f"rstd{r}"])
                    cmp("vector", lambda e, xt=xt, r=r, xn_ap=xn_ap: e.tensor_scalar(
                        xn_ap, xt, stat["mean"][r][:], stat["rstd"][r][:],
                        ALU.subtract, ALU.mult),
                        ["xst", f"mean{r}", f"rstd{r}"], [xn_tok])
                br = ib % 2
                for g in range(CG):
                    pr = g % 2
                    for i in range(4):
                        p = ib * 4 + i
                        xn_ap, xn_tok = xn_tile(p)
                        cmp("tensor", lambda e, g=g, i=i, pr=pr, xn_ap=xn_ap: e.transpose(
                            psA[pr][:, i * P:(i + 1) * P],
                            xn_ap[:, g * P:(g + 1) * P], ident[:]),
                            [xn_tok, "ident", f"psA{pr}"], [f"psA{pr}"])
                    cmp("scalar", lambda e, g=g, pr=pr, br=br: e.activation(
                        out=xfb[br][:, g * 512:(g + 1) * 512],
                        in_=psA[pr][:, 0:512], func=AF.Copy),
                        [f"psA{pr}"], [f"xfb{br}"])
                for kg in range(CG):
                    kr = kg % 2
                    for cg in range(CG):
                        cmp("tensor", lambda e, kg=kg, cg=cg, kr=kr, br=br: e.matmul(
                            psK[kr][:],
                            wk_bf[:, cg * C + kg * P: cg * C + (kg + 1) * P],
                            xfb[br][:, cg * 512:(cg + 1) * 512],
                            start=(cg == 0), stop=(cg == CG - 1)),
                            [f"xfb{br}", "wqk", f"psK{kr}"], [f"psK{kr}"])
                    cmp("vector", lambda e, kg=kg, kr=kr, ib=ib: e.tensor_scalar(
                        k_fm[:, kg * 4096 + ib * 512: kg * 4096 + (ib + 1) * 512],
                        psK[kr][:], b_sb[:, 8 + kg:9 + kg], None, ALU.add),
                        [f"psK{kr}", "b_sb"], ["k_fm"])
                if ib < 4:
                    for tch in range(4):
                        p = ib * 4 + tch
                        for cg in range(CG):
                            cmp("tensor", lambda e, tch=tch, cg=cg, br=br: e.matmul(
                                psZ[:, 0:E],
                                xfb[br][:, cg * 512 + tch * P: cg * 512 + (tch + 1) * P],
                                wr_bf[:, cg * E:(cg + 1) * E],
                                start=(cg == 0), stop=(cg == CG - 1)),
                                [f"xfb{br}", "wr", "psZ"], ["psZ"])
                        cmp("scalar", lambda e, p=p: e.activation(
                            out=z_sb[:, p * E:(p + 1) * E], in_=psZ[:, 0:E], func=AF.Copy),
                            ["psZ"], ["z_sb"])

            for p in range(NSLOT):
                r = p % 2
                z = z_sb[:, p * E:(p + 1) * E]
                cmp("vector", lambda e, z=z, r=r: e.reduce_max(
                    stat["m1"][r][:], z, axis=AX.X), ["z_sb"], [f"gm1{r}"])
                cmp("vector", lambda e, z=z, r=r: e.tensor_scalar(
                    g8["eq"][r][:], z, stat["m1"][r][:], None, ALU.is_equal),
                    ["z_sb", f"gm1{r}"], [f"geq{r}"])
                cmp("vector", lambda e, z=z, r=r: e.scalar_tensor_tensor(
                    g8["tmp"][r][:], g8["eq"][r][:], NEG, z, ALU.mult, ALU.add),
                    [f"geq{r}", "z_sb"], [f"gtmp{r}"])
                cmp("vector", lambda e, r=r: e.reduce_max(
                    stat["m2"][r][:], g8["tmp"][r][:], axis=AX.X),
                    [f"gtmp{r}"], [f"gm2{r}"])
                cmp("vector", lambda e, z=z, r=r: e.tensor_scalar(
                    g8["zs"][r][:], z, stat["m1"][r][:], None, ALU.subtract),
                    ["z_sb", f"gm1{r}"], [f"gzs{r}"])
                cmp("scalar", lambda e, r=r: e.activation(
                    out=g8["ex"][r][:], in_=g8["zs"][r][:], func=AF.Exp),
                    [f"gzs{r}"], [f"gex{r}"])
                cmp("vector", lambda e, z=z, r=r: e.tensor_scalar(
                    g8["ind"][r][:], z, stat["m2"][r][:], None, ALU.is_ge),
                    ["z_sb", f"gm2{r}"], [f"gind{r}"])
                cmp("vector", lambda e, r=r: e.tensor_tensor(
                    g8["es"][r][:], g8["ex"][r][:], g8["ind"][r][:], ALU.mult),
                    [f"gex{r}", f"gind{r}"], [f"ges{r}"])
                cmp("vector", lambda e, r=r: e.reduce_sum(
                    stat["gs"][r][:], g8["es"][r][:], axis=AX.X),
                    [f"ges{r}"], [f"ggs{r}"])
                cmp("vector", lambda e, r=r: e.reciprocal(
                    stat["grs"][r][:], stat["gs"][r][:]),
                    [f"ggs{r}"], [f"ggrs{r}"])
                cmp("vector", lambda e, p=p, r=r: e.tensor_scalar(
                    w_gate[:, p * E:(p + 1) * E], g8["es"][r][:], stat["grs"][r][:],
                    None, ALU.mult),
                    [f"ges{r}", f"ggrs{r}"], ["w_gate"])
            tri_s = x_stage[:, 0:P]
            ones_f = x_stage[0:1, P:2 * P]
            ones_c = x_stage[:, 2 * P:2 * P + 1]
            erow = x_stage[0:1, 272:272 + E]
            brow = x_stage[0:1, 288:288 + E]
            iab_f = x_stage[:, 640:672]
            sel8 = [x_stage[:, 672 + 8 * r:680 + 8 * r] for r in range(2)]
            nse8 = [x_stage[:, 688 + 8 * r:696 + 8 * r] for r in range(2)]
            tmp8r = [x_stage[:, 704 + 8 * r:712 + 8 * r] for r in range(2)]
            rtoks = (["xst", "tris", "onesf", "onesc", "erow", "brow", "dest", "iab"]
                     + [f"{n}{r}" for n in ("sel8", "nse8", "tmp8r") for r in range(2)])
            cmp("gpsimd", lambda e: e.memset(x_stage[:, 0:C], 0.0), rtoks, rtoks)
            cmp("gpsimd", lambda e: e.memset(ones_f, 1.0), ["onesf"], ["onesf"])
            cmp("gpsimd", lambda e: e.memset(ones_c, 1.0), ["onesc"], ["onesc"])
            cmp("gpsimd", lambda e: e.memset(tri_s, 0.0), ["tris"], ["tris"])
            cmp("gpsimd", lambda e: e.affine_select(
                out=tri_s, in_=tri_s, compare_op=ALU.is_ge, fill=1.0,
                base=0, pattern=[[-1, P]], channel_multiplier=1), ["tris"], ["tris"])
            for _e in range(E):
                cmp("gpsimd", lambda e, _e=_e: e.memset(erow[0:1, _e:_e + 1], float(_e * CAP)),
                    ["erow"], ["erow"])
            cmp("vector", lambda e: e.tensor_copy(brow, erow),
                ["erow", "brow"], ["brow"])
            for p in range(NSLOT):
                r = p % 2
                wcol = w_gate[:, p * E:(p + 1) * E]
                cmp("vector", lambda e, wcol=wcol, r=r: e.tensor_scalar(
                    sel8[r], wcol, 0.0, None, ALU.is_gt),
                    ["w_gate", f"sel8{r}"], [f"sel8{r}"])
                cmp("vector", lambda e, wcol=wcol, r=r: e.tensor_scalar(
                    nse8[r], wcol, 0.0, None, ALU.is_le),
                    ["w_gate", f"nse8{r}"], [f"nse8{r}"])
                cmp("tensor", lambda e, r=r: e.matmul(
                    psK[r][:, 0:E], tri_s, sel8[r], start=True, stop=False),
                    ["tris", f"sel8{r}", f"psK{r}"], [f"psK{r}"])
                cmp("tensor", lambda e, r=r: e.matmul(
                    psK[r][:, 0:E], ones_f, brow, start=False, stop=True),
                    ["onesf", "brow", f"psK{r}"], [f"psK{r}"])
                cmp("vector", lambda e, p=p, r=r: e.scalar_tensor_tensor(
                    dest_f[:, p * E:(p + 1) * E], nse8[r], BIG, psK[r][:, 0:E],
                    ALU.mult, ALU.add),
                    [f"nse8{r}", f"psK{r}", "dest"], ["dest"])
                cmp("tensor", lambda e, r=r: e.matmul(
                    psZ[0:1, 0:E], ones_c, sel8[r], start=True, stop=True),
                    ["onesc", f"sel8{r}", "psZ"], ["psZ"])
                cmp("vector", lambda e: e.tensor_tensor(
                    brow, brow, psZ[0:1, 0:E], ALU.add),
                    ["psZ", "brow"], ["brow"])
                cmp("vector", lambda e, p=p: e.tensor_reduce(
                    iab_f[:, p:p + 1], dest_f[:, p * E:(p + 1) * E], AX.X, ALU.min),
                    ["dest", "iab"], ["iab"])
                cmp("vector", lambda e, p=p, r=r: e.tensor_scalar(
                    tmp8r[r], dest_f[:, p * E:(p + 1) * E], iab_f[:, p:p + 1],
                    None, ALU.is_equal),
                    ["dest", "iab", f"tmp8r{r}"], [f"tmp8r{r}"])
                cmp("vector", lambda e, p=p, r=r: e.scalar_tensor_tensor(
                    tmp8r[r], tmp8r[r], BIG, dest_f[:, p * E:(p + 1) * E],
                    ALU.mult, ALU.add),
                    [f"tmp8r{r}", "dest"], [f"tmp8r{r}"])
                cmp("vector", lambda e, p=p, r=r: e.tensor_reduce(
                    iab_f[:, 16 + p:17 + p], tmp8r[r], AX.X, ALU.min),
                    [f"tmp8r{r}", "iab"], ["iab"])
            cmp("vector", lambda e: e.tensor_copy(iab_i[:], iab_f),
                ["iab", "iabi"], ["iabi"])
            if dbg:
                dma("gpsimd", d_zsb, z_sb[:], ["z_sb"], ["d1"])
                dma("gpsimd", d_wgate, w_gate[:], ["w_gate"], ["d2"])

        em.fence()
        with contextlib.ExitStack() as _stk:
            hw_all_h = _stk.enter_context(nc.sbuf_tensor("hw_all", [P, HEADS * C], BF16))

            probs_h = _stk.enter_context(nc.sbuf_tensor("probs", [P, 4096], BF16))
            probsB_h = _stk.enter_context(nc.sbuf_tensor("probsB", [P, 4096], BF16))
            pT0_h = _stk.enter_context(nc.sbuf_tensor("probsT0", [P, 512], BF16))
            pT1_h = _stk.enter_context(nc.sbuf_tensor("probsT1", [P, 512], BF16))
            pT2_h = _stk.enter_context(nc.sbuf_tensor("probsT2", [P, 512], BF16))
            y_slot_h = _stk.enter_context(nc.sbuf_tensor("y_slot", [P, C], F32))
            y_hb_h = _stk.enter_context(nc.sbuf_tensor("y_hb", [P, C], BF16))
            q0_h = _stk.enter_context(nc.sbuf_tensor("q_slot0", [P, C], BF16))
            q1_h = _stk.enter_context(nc.sbuf_tensor("q_slot1", [P, C], BF16))

            nmblk_h = _stk.enter_context(nc.sbuf_tensor("nmblk", [P, 16], F32))
            ebb_h = _stk.enter_context(nc.sbuf_tensor("ebb", [P, 16], F32))
            nM_h = _stk.enter_context(nc.sbuf_tensor("nM", [P, 2], F32))
            scv_h = _stk.enter_context(nc.sbuf_tensor("scv", [P, 16], F32))

            den_h = _stk.enter_context(nc.sbuf_tensor("den", [P, 2], F32))
            recip_h = _stk.enter_context(nc.sbuf_tensor("recip", [P, 2], F32))
            psS0_h = _stk.enter_context(nc.psum_tensor("psS0", [P, 512], F32))
            psS1_h = _stk.enter_context(nc.psum_tensor("psS1", [P, 512], F32))
            psS2_h = _stk.enter_context(nc.psum_tensor("psS2", [P, 512], F32))
            psT0_h = _stk.enter_context(nc.psum_tensor("psT0", [P, 512], BF16))
            psT1_h = _stk.enter_context(nc.psum_tensor("psT1", [P, 512], BF16))
            psT2_h = _stk.enter_context(nc.psum_tensor("psT2", [P, 512], BF16))
            psY0_h = _stk.enter_context(nc.psum_tensor("psY0", [P, 512], F32))
            psY1_h = _stk.enter_context(nc.psum_tensor("psY1", [P, 512], F32))

            hw_all = hw_all_h.ap()

            probs2 = [probs_h.ap(), probsB_h.ap()]
            probsT = [pT0_h.ap(), pT1_h.ap(), pT2_h.ap()]
            y_slot = y_slot_h.ap()
            y_hb2 = [y_hb_h.ap(), y_hb_h.ap()]
            q_slot = [q0_h.ap(), q1_h.ap()]
            _xfs = y_slot[:].bitcast(BF16)[:, 0:C]
            xfs = [_xfs, _xfs]
            nmblk = nmblk_h.ap()
            ebb = ebb_h.ap()
            nM = nM_h.ap()
            scv = scv_h.ap()
            sci = ebb
            den = den_h.ap()
            recip = recip_h.ap()
            psS = [psS0_h.ap(), psS1_h.ap(), psS2_h.ap()]
            psT = [psT0_h.ap(), psT1_h.ap(), psT2_h.ap()]
            psY = [psY0_h.ap(), psY1_h.ap()]
            psQ = psS[0]

            s_ctr = [0]

            def next_s():
                s_ctr[0] += 1
                return s_ctr[0] % 3

            hw_st = y_slot[0:1, 0:C]
            for h in range(HEADS):
                dma("gpsimd", hw_st, head_w[h:h + 1, :], ["hwst"], ["hwst", "y_slot"])
                for half in range(2):
                    sr = next_s()
                    cmp("tensor", lambda e, half=half, sr=sr: e.matmul(
                        psS[sr][:], ones_row[:], hw_st[:, half * 512:(half + 1) * 512],
                        start=True, stop=True),
                        ["hwst", "ones", "y_slot", f"psS{sr}"], [f"psS{sr}"])
                    cmp("scalar", lambda e, h=h, half=half, sr=sr: e.activation(
                        out=hw_all[:, h * C + half * 512: h * C + (half + 1) * 512],
                        in_=psS[sr][:], func=AF.Copy),
                        [f"psS{sr}"], ["hw_all"])

            def s_tiles(j):
                return list(range(0, j + 1)) + list(range(16, 16 + j + 1))

            def qk_blocks(j):
                blocks = []
                for run_start, run_len, phys0 in ((0, j + 1, 0), (j + 1, j + 1, 16)):
                    off = 0
                    while off < run_len:
                        n = min(4, run_len - off)
                        blocks.append((run_start + off, n, phys0 + off))
                        off += n
                return blocks

            def qk_matmul(j, h, pcol, n, kphys, sr, sp):
                qg, qrow = h // 2, (h % 2) * HS
                cmp("tensor", lambda e: e.matmul(
                    psS[sr][:, 0:n * P],
                    q_slot[sp][qrow:qrow + HS, qg * P:(qg + 1) * P],
                    k_fm[qrow:qrow + HS,
                         (h // 2) * 4096 + kphys * P: (h // 2) * 4096 + (kphys + n) * P],
                    start=True, stop=True),
                    [f"qs{sp}", "k_fm", f"psS{sr}"], [f"psS{sr}"])

            def s2a(j, h):
                hp = h % 2
                sp = j % 2
                probs = probs2[hp]
                ptok = f"probs{hp}"
                co = hp * 8
                blocks = qk_blocks(j)
                for bi, (pcol, n, kphys) in enumerate(blocks):
                    sr = next_s()
                    qk_matmul(j, h, pcol, n, kphys, sr, sp)
                    cmp("vector", lambda e, n=n, sr=sr, co=co, bi=bi: e.reduce_max(
                        nmblk[:, co + bi:co + bi + 1], psS[sr][:, 0:n * P], axis=AX.X,
                        negate=True),
                        [f"psS{sr}"], [f"nmblk{hp}"])
                    cmp("vector", lambda e, co=co, bi=bi: e.tensor_scalar_mul(
                        ebb[:, co + bi:co + bi + 1], nmblk[:, co + bi:co + bi + 1], 0.125),
                        [f"nmblk{hp}", f"ebb{hp}"], [f"ebb{hp}"])
                    cmp("scalar", lambda e, probs=probs, pcol=pcol, n=n, sr=sr,
                        co=co, bi=bi: e.activation(
                        out=probs[:, pcol * P:(pcol + n) * P], in_=psS[sr][:, 0:n * P],
                        func=AF.Exp, bias=ebb[:, co + bi:co + bi + 1], scale=0.125),
                        [f"psS{sr}", f"ebb{hp}", ptok], [ptok])
                nb = len(blocks)
                cmp("vector", lambda e, co=co, hp=hp: e.tensor_copy(
                    nM[:, hp:hp + 1], nmblk[:, co:co + 1]),
                    [f"nmblk{hp}", f"nM{hp}"], [f"nM{hp}"])
                for bi in range(1, nb):
                    cmp("vector", lambda e, co=co, bi=bi, hp=hp: e.tensor_tensor(
                        nM[:, hp:hp + 1], nM[:, hp:hp + 1], nmblk[:, co + bi:co + bi + 1],
                        ALU.min),
                        [f"nmblk{hp}", f"nM{hp}"], [f"nM{hp}"])
                cmp("vector", lambda e, co=co, nb=nb, hp=hp: e.tensor_tensor(
                    sci[:, co:co + nb], nM[:, hp:hp + 1].broadcast_to((P, nb)),
                    nmblk[:, co:co + nb], ALU.subtract),
                    [f"nM{hp}", f"nmblk{hp}", f"ebb{hp}"], [f"ebb{hp}"])
                cmp("scalar", lambda e, co=co, nb=nb: e.activation(
                    out=scv[:, co:co + nb], in_=sci[:, co:co + nb],
                    func=AF.Exp, scale=0.125),
                    [f"ebb{hp}", f"scv{hp}"], [f"scv{hp}"])
                for bi, (pcol, n, kphys) in enumerate(blocks):
                    cmp("vector", lambda e, probs=probs, pcol=pcol, n=n, co=co, bi=bi:
                        e.tensor_scalar(
                            probs[:, pcol * P:(pcol + n) * P],
                            probs[:, pcol * P:(pcol + n) * P],
                            scv[:, co + bi:co + bi + 1], None, ALU.mult),
                        [f"scv{hp}", ptok], [ptok])
                cmp("vector", lambda e, probs=probs, j=j: e.tensor_tensor(
                    probs[:, j * P:(j + 1) * P], probs[:, j * P:(j + 1) * P],
                    tri_bf[:], ALU.mult),
                    [ptok, "tri"], [ptok])
                cmp("vector", lambda e, probs=probs, j=j: e.tensor_scalar(
                    probs[:, (2 * j + 1) * P:(2 * j + 2) * P],
                    probs[:, (2 * j + 1) * P:(2 * j + 2) * P],
                    padv_sb[:], None, ALU.mult),
                    [ptok, "padv"], [ptok])
                nst = 2 * (j + 1)
                cmp("vector", lambda e, probs=probs, nst=nst, hp=hp: e.reduce_sum(
                    den[:, hp:hp + 1], probs[:, 0:nst * P], axis=AX.X),
                    [ptok], [f"den{hp}"])
                cmp("vector", lambda e, hp=hp: e.reciprocal(
                    recip[:, hp:hp + 1], den[:, hp:hp + 1]),
                    [f"den{hp}"], [f"recip{hp}"])

            def s2b(j, h, part):
                hp = h % 2
                probs = probs2[hp]
                ptok = f"probs{hp}"
                tiles = s_tiles(j)
                nst = len(tiles)
                groups = list(range(0, nst, 4))

                def emit_T(g0):
                    gn = min(4, nst - g0)
                    tr = (g0 // 4) % 3
                    for i in range(gn):
                        st = g0 + i
                        cmp("tensor", lambda e, probs=probs, st=st, i=i, tr=tr: e.transpose(
                            psT[tr][:, i * P:(i + 1) * P],
                            probs[:, st * P:(st + 1) * P], ident[:]),
                            [ptok, "ident", f"psT{tr}"], [f"psT{tr}"])
                    cmp("scalar", lambda e, gn=gn, tr=tr: e.activation(
                        out=probsT[tr][:, 0:gn * P], in_=psT[tr][:, 0:gn * P],
                        func=AF.Copy),
                        [f"psT{tr}"], [f"pT{tr}"])

                def emit_AV(g0):
                    gn = min(4, nst - g0)
                    tr = (g0 // 4) % 3
                    for i in range(gn):
                        st = g0 + i
                        phys = tiles[st]
                        xn_ap, xn_tok = xn_tile(phys)
                        first, last = (st == 0), (st == nst - 1)
                        for half in range(2):
                            cmp("tensor",
                                lambda e, i=i, tr=tr, half=half, xn_ap=xn_ap,
                                first=first, last=last: e.matmul(
                                    psY[half][:], probsT[tr][:, i * P:(i + 1) * P],
                                    xn_ap[:, half * 512:(half + 1) * 512],
                                    start=first, stop=last),
                                [f"pT{tr}", xn_tok, f"psY{half}"], [f"psY{half}"])

                if part == 0:
                    for g0 in groups:
                        emit_T(g0)
                    return
                for g0 in groups:
                    emit_AV(g0)
                y_hb = y_hb2[hp]
                ytok = "y_h"
                for half in range(2):
                    cmp("scalar", lambda e, y_hb=y_hb, half=half, hp=hp: e.activation(
                        out=y_hb[:, half * 512:(half + 1) * 512], in_=psY[half][:],
                        func=AF.Copy, scale=recip[:, hp:hp + 1]),
                        [f"psY{half}", f"recip{hp}", ytok], [ytok])
                cmp("vector", lambda e, y_hb=y_hb, h=h: e.tensor_tensor(
                    y_hb[:], y_hb[:], hw_all[:, h * C:(h + 1) * C], ALU.mult),
                    [ytok, "hw_all"], [ytok])
                if h == 0:
                    cmp("vector", lambda e, y_hb=y_hb: e.tensor_copy(y_slot[:], y_hb[:]),
                        [ytok, "y_slot"], ["y_slot"])
                else:
                    cmp("vector", lambda e, y_hb=y_hb: e.tensor_tensor(
                        y_slot[:], y_slot[:], y_hb[:], ALU.add),
                        [ytok, "y_slot"], ["y_slot"])

            for j in range(NSLOT):
                sp = j % 2
                for g in range(CG):
                    tr = g % 2
                    cmp("tensor", lambda e, g=g, j=j, tr=tr: e.transpose(
                        psT[tr][:, (g % 4) * P:((g % 4) + 1) * P],
                        xn_own[:, j * C + g * P: j * C + (g + 1) * P], ident[:]),
                        [f"xn{j}", "ident", f"psT{tr}"], [f"psT{tr}"])
                    cmp("scalar", lambda e, g=g, tr=tr, sp=sp: e.activation(
                        out=xfs[sp][:, g * P:(g + 1) * P],
                        in_=psT[tr][:, (g % 4) * P:((g % 4) + 1) * P], func=AF.Copy),
                        [f"psT{tr}", f"xfs{sp}", "y_slot"], [f"xfs{sp}", "y_slot"])
                for rnd in range(2):
                    for qi in range(4):
                        qg = rnd * 4 + qi
                        for cg in range(CG):
                            cmp("tensor", lambda e, qg=qg, qi=qi, cg=cg, sp=sp: e.matmul(
                                psQ[:, qi * P:(qi + 1) * P],
                                wq_bf[:, cg * C + qg * P: cg * C + (qg + 1) * P],
                                xfs[sp][:, cg * P:(cg + 1) * P],
                                start=(cg == 0), stop=(cg == CG - 1)),
                                [f"xfs{sp}", "y_slot", "wqk", "psS0"], ["psS0"])
                    for qi in range(4):
                        qg = rnd * 4 + qi
                        cmp("vector", lambda e, qg=qg, qi=qi, sp=sp: e.tensor_scalar(
                            q_slot[sp][:, qg * P:(qg + 1) * P],
                            psQ[:, qi * P:(qi + 1) * P], b_sb[:, qg:qg + 1], None, ALU.add),
                            ["psS0", "b_sb", f"qs{sp}"], [f"qs{sp}"])
                for t in range(HEADS + 1):
                    if t >= 1:
                        s2b(j, t - 1, 0)
                    if t < HEADS:
                        s2a(j, t)
                    if t >= 1:
                        s2b(j, t - 1, 1)
                dma("gpsimd", y_attn[j * P:(j + 1) * P, :], y_slot[:],
                    ["y_slot"], [f"ya{j}"])
            if dbg:
                dma("gpsimd", d_xnown, xn_own[:], [f"xn{p}" for p in range(16)], ["d3"])
                dma("gpsimd", d_xnoth, xn_oth[:], [f"xn{p}" for p in range(16, 32)], ["d4"])
                dma("gpsimd", d_kfm, k_fm[:], ["k_fm"], ["d5"])
                dma("gpsimd", d_probs, probs2[1][:], ["probs1"], ["d6"])
                dma("gpsimd", d_hwall, hw_all[:], ["hw_all"], ["d7"])
                dma("gpsimd", d_qslot, q_slot[1][:], ["qs1"], ["d8"])
                cmp("gpsimd", lambda e: e.tensor_copy(
                    d_small[:, 0:8] if False else d_small[:, 0:8], d_small[:, 0:8])
                    if False else e.engine_nop(), [], [])
                dma("gpsimd", d_small[:, 0:2], nrun[:], ["nrun0", "nrun1"], ["d9"])
                dma("gpsimd", d_small[:, 2:4], den[:], ["den0", "den1"], ["d10"])
                dma("gpsimd", d_small[:, 4:6], recip[:], ["recip0", "recip1"], ["d11"])
                dma("gpsimd", d_small[:, 6:8], ebias[:], ["ebias0", "ebias1"], ["d12"])

    em.fence()
    with contextlib.ExitStack() as _stk:
        y_acc_h = _stk.enter_context(nc.sbuf_tensor("y_acc", [P, NSLOT * C], F32))
        yp_sb_h = _stk.enter_context(nc.sbuf_tensor("yp_sb", [P, NTP * C], BF16))
        xp_fm_h = _stk.enter_context(nc.sbuf_tensor("xp_fm", [P, CG * CAP], BF16))
        xp_fmB_h = _stk.enter_context(nc.sbuf_tensor("xp_fmB", [P, CG * CAP], BF16))
        w1_bf_h = _stk.enter_context(nc.sbuf_tensor("w1_bf", [P, CG * C], BF16))
        w2_bf_h = _stk.enter_context(nc.sbuf_tensor("w2_bf", [P, CG * C], BF16))
        wstc0_h = _stk.enter_context(nc.sbuf_tensor("wstc0", [P, 2 * C], F32))
        wstc1_h = _stk.enter_context(nc.sbuf_tensor("wstc1", [P, 2 * C], F32))
        hid0_h = _stk.enter_context(nc.sbuf_tensor("hid0", [P, 4096], BF16))
        hid1_h = _stk.enter_context(nc.sbuf_tensor("hid1", [P, 2048], BF16))
        xpt0_h = _stk.enter_context(nc.sbuf_tensor("xpt0", [P, C], BF16))
        xpt1_h = _stk.enter_context(nc.sbuf_tensor("xpt1", [P, C], BF16))
        prm0_h = _stk.enter_context(nc.sbuf_tensor("prm0", [P, P], BF16))
        prm1_h = _stk.enter_context(nc.sbuf_tensor("prm1", [P, P], BF16))
        dcl_h = _stk.enter_context(nc.sbuf_tensor("dcl", [P, 2], F32))
        iota_t_h = _stk.enter_context(nc.sbuf_tensor("iota_t", [P, P], BF16))

        psM0_h = _stk.enter_context(nc.psum_tensor("psM0", [P, 512], F32))
        psM1_h = _stk.enter_context(nc.psum_tensor("psM1", [P, 512], F32))
        psW0a_h = _stk.enter_context(nc.psum_tensor("psW0a", [P, 512], F32))
        psW0b_h = _stk.enter_context(nc.psum_tensor("psW0b", [P, 512], F32))
        psW1a_h = _stk.enter_context(nc.psum_tensor("psW1a", [P, 512], F32))
        psW1b_h = _stk.enter_context(nc.psum_tensor("psW1b", [P, 512], F32))
        psX0_h = _stk.enter_context(nc.psum_tensor("psX0", [P, 1024], BF16))
        psX1_h = _stk.enter_context(nc.psum_tensor("psX1", [P, 1024], BF16))
        y_acc = y_acc_h.ap()
        yp_sb = yp_sb_h.ap()
        xp_fm2 = [xp_fm_h.ap(), xp_fmB_h.ap()]
        w1_bf2 = [w1_bf_h.ap(), w1_bf_h.ap()]
        w2_bf2 = [w2_bf_h.ap(), w2_bf_h.ap()]
        wstc = [wstc0_h.ap(), wstc1_h.ap()]
        hid = [hid0_h.ap(), hid1_h.ap()]
        xpt = [xpt0_h.ap(), xpt1_h.ap()]
        prm = [prm0_h.ap(), prm1_h.ap()]
        dcl = dcl_h.ap()
        iota_t = iota_t_h.ap()
        ya_st = [hid[0].bitcast(F32)[:, 0:C], hid[1].bitcast(F32)[:, 0:C]]
        psM = [psM0_h.ap(), psM1_h.ap()]
        psW = [[psW0a_h.ap(), psW0b_h.ap()], [psW1a_h.ap(), psW1b_h.ap()]]
        psX = [psX0_h.ap(), psX1_h.ap()]
        YPALL = [f"yp{e_}" for e_ in range(E)]

        dma("gpsimd", iota_t[:], iota01, [], ["iota"])
        for p in range(NSLOT):
            r = p % 2
            dma("gpsimd", y_acc[:, p * C:(p + 1) * C], x_seq[p * P:(p + 1) * P, :],
                [f"yac{p}"], [f"yac{p}"])
            dma("gpsimd", ya_st[r], y_attn[p * P:(p + 1) * P, :],
                [f"ya{p}", f"yast{r}", f"hid{r}"], [f"yast{r}", f"hid{r}"])
            cmp("vector", lambda e, p=p, r=r: e.tensor_tensor(
                y_acc[:, p * C:(p + 1) * C], y_acc[:, p * C:(p + 1) * C],
                ya_st[r], ALU.add),
                [f"yast{r}", f"hid{r}", f"yac{p}"], [f"yac{p}"])
        if dbg:
            dma("gpsimd", d_yinit, y_acc[:], [f"yac{p}" for p in range(NSLOT)], ["d18"])

        def xp_prep(e_, t6_list=None):
            ep = e_ % 2
            xp_fm = xp_fm2[ep]
            for t6 in (range(NTP) if t6_list is None else t6_list):
                r = t6 % 2
                for p in range(NSLOT):
                    pr = p % 2
                    cmp("vector", lambda e, p=p, pr=pr, e_=e_, t6=t6: e.tensor_scalar(
                        dcl[:, pr:pr + 1], dest_f[:, p * E + e_: p * E + e_ + 1],
                        -float(e_ * CAP + t6 * P), None, ALU.add),
                        ["dest", f"dcl{pr}"], [f"dcl{pr}"])
                    cmp("vector", lambda e, p=p, pr=pr, e_=e_: e.tensor_scalar(
                        prm[pr][:], iota_t[:], dcl[:, pr:pr + 1],
                        w_gate[:, p * E + e_: p * E + e_ + 1],
                        ALU.is_equal, ALU.mult),
                        ["iota", f"dcl{pr}", "w_gate", f"prm{pr}"], [f"prm{pr}"])
                    for half in range(2):
                        cmp("tensor", lambda e, p=p, pr=pr, half=half: e.matmul(
                            psM[half][:],
                            prm[pr][:],
                            xn_own[:, p * C + half * 512: p * C + (half + 1) * 512],
                            start=(p == 0), stop=(p == NSLOT - 1)),
                            [f"prm{pr}", f"xn{p}", f"psM{half}"], [f"psM{half}"])
                for half in range(2):
                    cmp("scalar", lambda e, r=r, half=half: e.activation(
                        out=xpt[r][:, half * 512:(half + 1) * 512], in_=psM[half][:],
                        func=AF.Copy),
                        [f"psM{half}", f"xpt{r}"], [f"xpt{r}"])
                for g in range(CG):
                    cmp("tensor", lambda e, g=g, r=r: e.transpose(
                        psX[r][:, g * P:(g + 1) * P],
                        xpt[r][:, g * P:(g + 1) * P], ident[:]),
                        [f"xpt{r}", "ident", f"psX{r}"], [f"psX{r}"])
                cmp("scalar", lambda e, xp_fm=xp_fm, t6=t6, r=r: e.activation(
                    out=xp_fm[:].rearrange("q (g t) -> q g t", g=CG)[:, :, t6 * P:(t6 + 1) * P],
                    in_=psX[r][:].rearrange("q (g t) -> q g t", g=CG),
                    func=AF.Copy),
                    [f"psX{r}", f"xpfm{ep}"], [f"xpfm{ep}"])

        def load_w(idx, e_, fc):
            wp = idx % 2
            w1_bf = w1_bf2[wp]
            w2_bf = w2_bf2[wp]
            for cg2 in range(CG // 2):
                r = cg2 % 2
                dma("sync", wstc[r][:].rearrange("q (g c) -> q g c", g=2),
                    w1_in[e_, cg2 * 2 * P:(cg2 * 2 + 2) * P, fc * C:(fc + 1) * C]
                    .rearrange("(g p) c -> p g c", p=P),
                    [f"wstc{r}"], [f"wstc{r}"])
                cmp("scalar", lambda e, w1_bf=w1_bf, r=r, cg2=cg2: e.activation(
                    out=w1_bf[:, cg2 * 2 * C:(cg2 * 2 + 2) * C], in_=wstc[r][:],
                    func=AF.Copy),
                    [f"wstc{r}", "w1bf"], ["w1bf"])
            for fs2 in range(CG // 2):
                r = fs2 % 2
                dma("gpsimd", wstc[r][:].rearrange("q (g c) -> q g c", g=2),
                    w2_in[e_, fc * C + fs2 * 2 * P: fc * C + (fs2 * 2 + 2) * P, :]
                    .rearrange("(g p) c -> p g c", p=P),
                    [f"wstc{r}"], [f"wstc{r}"])
                cmp("scalar", lambda e, w2_bf=w2_bf, r=r, fs2=fs2: e.activation(
                    out=w2_bf[:, fs2 * 2 * C:(fs2 * 2 + 2) * C], in_=wstc[r][:],
                    func=AF.Copy),
                    [f"wstc{r}", "w2bf"], ["w2bf"])

        TBS = [(0, 512), (512, 256)]
        pairs = [(e_, fc) for e_ in range(E) for fc in range(4)]
        load_w(0, *pairs[0])
        xp_prep(0)
        for idx, (e_, fc) in enumerate(pairs):
            wp = idx % 2
            ep = e_ % 2
            w1_bf = w1_bf2[wp]
            w2_bf = w2_bf2[wp]
            xp_fm = xp_fm2[ep]
            for tbi, (tb_off, tbsz) in enumerate(TBS):
                hr = tbi % 2
                for fs in range(CG):
                    mr = fs % 2
                    for cg in range(CG):
                        cmp("tensor", lambda e, w1_bf=w1_bf, xp_fm=xp_fm, fs=fs, cg=cg,
                            mr=mr, tb_off=tb_off, tbsz=tbsz: e.matmul(
                                psM[mr][:, 0:tbsz],
                                w1_bf[:, cg * C + fs * P: cg * C + (fs + 1) * P],
                                xp_fm[:, cg * CAP + tb_off: cg * CAP + tb_off + tbsz],
                                start=(cg == 0), stop=(cg == CG - 1)),
                            [f"xpfm{ep}", "w1bf", f"psM{mr}"], [f"psM{mr}"])
                    cmp("scalar", lambda e, fs=fs, mr=mr, hr=hr, tbsz=tbsz: e.activation(
                        out=hid[hr][:, fs * tbsz: (fs + 1) * tbsz],
                        in_=psM[mr][:, 0:tbsz], func=AF.Relu),
                        [f"psM{mr}", f"hid{hr}"], [f"hid{hr}"])
                for i in range(tbsz // P):
                    gi = tb_off // P + i
                    wr_ = gi % 2
                    for fs in range(CG):
                        for half in range(2):
                            cmp("tensor",
                                lambda e, w2_bf=w2_bf, fs=fs, i=i, half=half, wr_=wr_,
                                hr=hr, tbsz=tbsz: e.matmul(
                                    psW[wr_][half][:],
                                    hid[hr][:, fs * tbsz + i * P: fs * tbsz + (i + 1) * P],
                                    w2_bf[:, fs * C + half * 512: fs * C + (half + 1) * 512],
                                    start=(fs == 0), stop=(fs == CG - 1)),
                                [f"hid{hr}", "w2bf", f"psW{wr_}{half}"],
                                [f"psW{wr_}{half}"])
                    for half in range(2):
                        dst = yp_sb[:, gi * C + half * 512: gi * C + (half + 1) * 512]
                        if fc == 0:
                            cmp("vector", lambda e, dst=dst, half=half, wr_=wr_:
                                e.tensor_copy(dst, psW[wr_][half][:]),
                                [f"psW{wr_}{half}", "ypsb"], ["ypsb"])
                        else:
                            cmp("vector", lambda e, dst=dst, half=half, wr_=wr_:
                                e.tensor_tensor(dst, dst, psW[wr_][half][:], ALU.add),
                                [f"psW{wr_}{half}", "ypsb"], ["ypsb"])
            if fc == 3:
                for t6 in range(NTP):
                    dma("gpsimd", yp[e_ * CAP + t6 * P: e_ * CAP + (t6 + 1) * P, :],
                        yp_sb[:, t6 * C:(t6 + 1) * C],
                        ["ypsb"], [f"yp{e_}"])
            if idx + 1 < len(pairs):
                load_w(idx + 1, *pairs[idx + 1])
            if e_ + 1 < E:
                xp_prep(e_ + 1, [[0, 1], [2, 3], [4], [5]][fc])

        for p in range(NSLOT):
            for k in range(2):
                r = (2 * p + k) % 2
                col = p if k == 0 else 16 + p
                em.emit("gpsimd", lambda e, r=r, col=col: e.indirect_dma_start(
                    xpt[r][:], None,
                    yp[0:E * CAP, :],
                    bass.IndirectOffsetOnAxis(iab_i[:, col:col + 1], 0)),
                    reads=YPALL + ["iabi", f"xpt{r}"], writes=[f"xpt{r}"],
                    kind="dma")
                cmp("vector", lambda e, p=p, r=r: e.tensor_tensor(
                    y_acc[:, p * C:(p + 1) * C], y_acc[:, p * C:(p + 1) * C],
                    xpt[r][:], ALU.add),
                    [f"xpt{r}", f"yac{p}"], [f"yac{p}"])

        for p in range(NSLOT):
            dma("gpsimd", y_out[p * P:(p + 1) * P, :], y_acc[:, p * C:(p + 1) * C],
                [f"yac{p}"], [f"out{p}"])
        cmp("gpsimd", lambda e: e.engine_nop(),
            [f"out{p}" for p in range(NSLOT)]
            + ([f"d{i}" for i in range(1, 13)] + ["d18"]
               + [f"ya{j}" for j in range(NSLOT)] if dbg else []), [])

    em.finalize()

    with _sem_stack(nc, em) as sems:
        with nc.Block() as block:
            @block.sync
            def _(eng):
                em.replay("sync", eng, sems)

            @block.scalar
            def _(eng):
                em.replay("scalar", eng, sems)

            @block.vector
            def _(eng):
                em.replay("vector", eng, sems)

            @block.tensor
            def _(eng):
                em.replay("tensor", eng, sems)

            @block.gpsimd
            def _(eng):
                em.replay("gpsimd", eng, sems)

    return nc


_CACHED = {}


def kernel(L0_attn_x, L0_attn_W_attn, L0_attn_b_attn, L0_attn_head_weights,
           L0_ffn_Wr, L0_ffn_W1, L0_ffn_W2):
    x = np.ascontiguousarray(np.asarray(L0_attn_x, dtype=np.float32))
    W_attn = np.ascontiguousarray(np.asarray(L0_attn_W_attn, dtype=np.float32))
    b_att = np.ascontiguousarray(np.asarray(L0_attn_b_attn, dtype=np.float32))
    head_w = np.ascontiguousarray(np.asarray(L0_attn_head_weights, dtype=np.float32))
    Wr = np.ascontiguousarray(np.asarray(L0_ffn_Wr, dtype=np.float32))
    W1 = np.ascontiguousarray(np.asarray(L0_ffn_W1, dtype=np.float32))
    W2 = np.ascontiguousarray(np.asarray(L0_ffn_W2, dtype=np.float32))

    import ml_dtypes
    tri = np.where(np.arange(P)[:, None] >= np.arange(P)[None, :],
                   1.0, 0.0).astype(ml_dtypes.bfloat16)
    iota = np.tile(np.arange(P, dtype=np.float32), (P, 1)).astype(ml_dtypes.bfloat16)

    if "nc" not in _CACHED:
        _CACHED["nc"] = build_program()
    nc = _CACHED["nc"]

    in_maps = []
    for c in range(8):
        b, r = c % 4, c // 4
        own = [2 * p + r for p in range(NSLOT)]
        oth = [2 * p + (1 - r) for p in range(NSLOT)]
        rows = np.concatenate([np.arange(t * P, (t + 1) * P) for t in own + oth])
        x_seq_c = np.ascontiguousarray(x[b][rows])
        padval = np.full((P, 1), 0.0 if r == 0 else 1.0, dtype=np.float32)
        in_maps.append(dict(
            x_seq=x_seq_c, w_attn=W_attn, b_attn=b_att, head_w=head_w,
            wr_in=Wr, w1_in=W1, w2_in=W2, tri01=tri, iota01=iota, padv=padval,
        ))

    res = run_bass_kernel_spmd(nc, in_maps, core_ids=list(range(8)))

    out = np.empty((4, 4096, C), dtype=np.float32)
    for c in range(8):
        b, r = c % 4, c // 4
        yo = np.asarray(res.results[c]["y_out"], dtype=np.float32)
        for p in range(NSLOT):
            t = 2 * p + r
            out[b, t * P:(t + 1) * P, :] = yo[p * P:(p + 1) * P, :]
    return out


# version 22
# speedup vs baseline: 1.0334x; 1.0334x over previous
import contextlib
import numpy as np
import concourse.bass as bass
import concourse.mybir as mybir
from concourse.bass_utils import run_bass_kernel_spmd

F32 = mybir.dt.float32
BF16 = mybir.dt.bfloat16
AF = mybir.ActivationFunctionType
ALU = mybir.AluOpType
AX = mybir.AxisListType

P = 128
C = 1024
CG = 8
T_CORE = 2048
NSLOT = 16
HEADS = 16
HS = 64
E = 8
NEG = -1e9
CAP = 768
NTP = CAP // P
BIG = 1.0e6
I32 = mybir.dt.int32


class Op:
    __slots__ = ("idx", "engine", "fn", "kind", "deps", "needs_inc", "sem", "val")

    def __init__(self, idx, engine, fn, kind):
        self.idx = idx
        self.engine = engine
        self.fn = fn
        self.kind = kind
        self.deps = []
        self.needs_inc = False
        self.sem = None
        self.val = 0


class Emitter:
    ENGINES = ("sync", "scalar", "vector", "tensor", "gpsimd")

    def __init__(self, keep_pe_edges=False):
        self.ops = []
        self.last_writer = {}
        self.readers = {}
        self.keep_pe_edges = keep_pe_edges
        self.last_on = {}
        self.last_dma_slot = {}
        self.dma_count = {}
        self.pending_fence = {}

    def fence(self):
        snap = list(self.last_on.values()) + list(self.last_dma_slot.values())
        for e in self.ENGINES:
            self.pending_fence[e] = list(snap)

    def emit(self, engine, fn, reads=(), writes=(), kind="cmp"):
        op = Op(len(self.ops), engine, fn, kind)
        deps = set()
        pf = self.pending_fence.pop(engine, None)
        if pf:
            deps.update(pf)
        for t in reads:
            w = self.last_writer.get(t)
            if w is not None:
                deps.add(w)
            self.readers.setdefault(t, []).append(op)
        for t in writes:
            w = self.last_writer.get(t)
            if w is not None:
                deps.add(w)
            for r in self.readers.get(t, ()):
                deps.add(r)
            self.last_writer[t] = op
            self.readers[t] = []
        deps.discard(op)
        if self.keep_pe_edges:
            op.deps = list(deps)
        else:
            op.deps = [d for d in deps
                       if not (d.engine == "tensor" and engine == "tensor"
                               and d.kind == "cmp" and kind == "cmp")]
        self.ops.append(op)
        self.last_on[engine] = op
        if kind == "dma":
            s = self.dma_count.get(engine, 0)
            self.dma_count[engine] = s + 1
            self.last_dma_slot[(engine, s % self.NDMASLOT)] = op
        return op

    NDMASLOT = 8

    def finalize(self):
        for op in self.ops:
            for d in op.deps:
                d.needs_inc = True
        counters = {}
        slot_rr = {}
        for op in self.ops:
            if op.kind == "dma":
                s = slot_rr.get(op.engine, 0)
                slot_rr[op.engine] = (s + 1) % self.NDMASLOT
                key = (op.engine, "dma", s)
                counters[key] = counters.get(key, 0) + 16
                op.sem = key
                op.val = counters[key]
            elif op.needs_inc:
                key = (op.engine, op.kind)
                counters[key] = counters.get(key, 0) + 1
                op.sem = key
                op.val = counters[key]
        self.sem_keys = sorted(set(op.sem for op in self.ops if op.sem is not None))
        self.final_counts = dict(counters)

    def replay(self, engine_name, eng, sems):
        watermark = {}
        for op in self.ops:
            if op.engine != engine_name:
                continue
            need = {}
            for d in op.deps:
                if d.sem is None:
                    continue
                if d.val > need.get(d.sem, 0):
                    need[d.sem] = d.val
            if op.kind == "dma" and op.val > 16:
                if op.val - 16 > need.get(op.sem, 0):
                    need[op.sem] = op.val - 16
            for k in sorted(need, key=str):
                v = need[k]
                if watermark.get(k, 0) >= v:
                    continue
                eng.wait_ge(sems[k], v)
                watermark[k] = v
            ins = op.fn(eng)
            if op.needs_inc:
                ins.then_inc(sems[op.sem], 16 if op.kind == "dma" else 1)


@contextlib.contextmanager
def _sem_stack(nc, em):
    with contextlib.ExitStack() as st:
        sems = {}
        for key in em.sem_keys:
            sems[key] = st.enter_context(nc.semaphore("s_" + "_".join(map(str, key))))
        yield sems


def build_program(dbg=False, keep_pe_edges=False):
    nc = bass.Bass("TRN2", target_bir_lowering=False, debug=False)

    x_seq = nc.dram_tensor("x_seq", [4096, C], F32, kind="ExternalInput").ap()
    w_attn = nc.dram_tensor("w_attn", [C, 2 * C], F32, kind="ExternalInput").ap()
    b_attn = nc.dram_tensor("b_attn", [2 * C], F32, kind="ExternalInput").ap()
    head_w = nc.dram_tensor("head_w", [HEADS, C], F32, kind="ExternalInput").ap()
    wr_in = nc.dram_tensor("wr_in", [C, E], F32, kind="ExternalInput").ap()
    w1_in = nc.dram_tensor("w1_in", [E, C, 4096], F32, kind="ExternalInput").ap()
    w2_in = nc.dram_tensor("w2_in", [E, 4096, C], F32, kind="ExternalInput").ap()
    tri01 = nc.dram_tensor("tri01", [P, P], BF16, kind="ExternalInput").ap()
    iota01 = nc.dram_tensor("iota01", [P, P], BF16, kind="ExternalInput").ap()
    padv = nc.dram_tensor("padv", [P, 1], F32, kind="ExternalInput").ap()

    y_out = nc.dram_tensor("y_out", [T_CORE, C], F32, kind="ExternalOutput").ap()
    y_attn = nc.dram_tensor("y_attn_scratch", [T_CORE, C], F32,
                            kind="ExternalOutput" if dbg else "Internal").ap()
    yp = nc.dram_tensor("yp_scratch", [E * CAP, C], BF16, kind="Internal").ap()
    if dbg:
        d_xnown = nc.dram_tensor("d_xnown", [P, NSLOT * C], BF16, kind="ExternalOutput").ap()
        d_xnoth = nc.dram_tensor("d_xnoth", [P, NSLOT * C], BF16, kind="ExternalOutput").ap()
        d_kfm = nc.dram_tensor("d_kfm", [P, 8 * 4096], BF16, kind="ExternalOutput").ap()
        d_probs = nc.dram_tensor("d_probs", [P, 4096], BF16, kind="ExternalOutput").ap()
        d_hwall = nc.dram_tensor("d_hwall", [P, HEADS * C], BF16, kind="ExternalOutput").ap()
        d_qslot = nc.dram_tensor("d_qslot", [P, C], BF16, kind="ExternalOutput").ap()
        d_zsb = nc.dram_tensor("d_zsb", [P, NSLOT * E], F32, kind="ExternalOutput").ap()
        d_wgate = nc.dram_tensor("d_wgate", [P, NSLOT * E], F32, kind="ExternalOutput").ap()
        d_small = nc.dram_tensor("d_small", [P, 8], F32, kind="ExternalOutput").ap()
        d_xnefm = nc.dram_tensor("d_xnefm", [P, CG * T_CORE], BF16, kind="ExternalOutput").ap()
        d_w1bf = nc.dram_tensor("d_w1bf", [P, CG * C], BF16, kind="ExternalOutput").ap()
        d_w2bf = nc.dram_tensor("d_w2bf", [P, CG * C], BF16, kind="ExternalOutput").ap()
        d_hid = nc.dram_tensor("d_hid", [P, 2 * 4096], BF16, kind="ExternalOutput").ap()
        d_yinit = nc.dram_tensor("d_yinit", [P, NSLOT * C], F32, kind="ExternalOutput").ap()
        d_yacc0 = nc.dram_tensor("d_yacc0", [P, NSLOT * C], F32, kind="ExternalOutput").ap()
        d_xnefm0 = nc.dram_tensor("d_xnefm0", [P, CG * T_CORE], BF16, kind="ExternalOutput").ap()
        d_hid00 = nc.dram_tensor("d_hid00", [P, 4096], BF16, kind="ExternalOutput").ap()
        d_w1bf0 = nc.dram_tensor("d_w1bf0", [P, CG * C], BF16, kind="ExternalOutput").ap()
        d_w2bf0 = nc.dram_tensor("d_w2bf0", [P, CG * C], BF16, kind="ExternalOutput").ap()

    em = Emitter(keep_pe_edges=keep_pe_edges)

    def dma(queue, out, in_, reads, writes, **kw):
        em.emit(queue, lambda e, o=out, i=in_, kw=kw: e.dma_start(out=o, in_=i, **kw),
                reads=reads, writes=writes, kind="dma")

    def cmp(engine, fn, reads, writes):
        em.emit(engine, fn, reads=reads, writes=writes, kind="cmp")

    ga = nc.alloc_sbuf_tensor

    xn_own = ga("xn_own", [P, NSLOT * C], BF16).ap()
    b_sb = ga("b_sb", [P, 16], F32).ap()
    wr_bf = ga("wr_bf", [P, 64], BF16).ap()
    w_gate = ga("w_gate", [P, NSLOT * E], F32).ap()
    ident = ga("ident", [P, P], BF16).ap()
    tri_bf = ga("tri_bf", [P, P], BF16).ap()
    padv_sb = ga("padv_sb", [P, 1], F32).ap()
    eps_t = ga("eps_t", [P, 1], F32).ap()
    ones_row = ga("ones_row", [1, P], F32).ap()

    dest_f = ga("dest_f", [P, NSLOT * E], F32).ap()
    iab_i = ga("iab_i", [P, 32], I32).ap()

    cmp("gpsimd", lambda e: e.memset(eps_t[:], 1e-5), [], ["eps"])
    cmp("gpsimd", lambda e: e.memset(ones_row[:], 1.0), [], ["ones"])
    cmp("gpsimd", lambda e: e.memset(ident[:], 0.0), [], ["ident"])
    cmp("gpsimd", lambda e: e.affine_select(
        out=ident[:], in_=ident[:], compare_op=ALU.not_equal, fill=1.0,
        base=0, pattern=[[-1, P]], channel_multiplier=1), ["ident"], ["ident"])
    dma("gpsimd", tri_bf[:], tri01, [], ["tri"])
    dma("gpsimd", padv_sb[:], padv, [], ["padv"])
    dma("gpsimd", b_sb[:], b_attn.rearrange("(g p) -> p g", p=P), [], ["b_sb"],
        allow_slow_non_contiguous=True)

    with contextlib.ExitStack() as _stk:
        xn_oth_h = _stk.enter_context(nc.sbuf_tensor("xn_oth", [P, 16 * C], BF16))
        k_fm_h = _stk.enter_context(nc.sbuf_tensor("k_fm", [P, 8 * 4096], BF16))
        wq_bf_h = _stk.enter_context(nc.sbuf_tensor("wq_bf", [P, 8 * C], BF16))
        xn_oth = xn_oth_h.ap()
        k_fm = k_fm_h.ap()
        wq_bf = wq_bf_h.ap()

        def xn_tile(p):
            if p < 16:
                return xn_own[:, p * C:(p + 1) * C], f"xn{p}"
            return xn_oth[:, (p - 16) * C:(p - 15) * C], f"xn{p}"

        with contextlib.ExitStack() as _stk:
            x_stage_h = _stk.enter_context(nc.sbuf_tensor("x_stage", [P, 4 * C], F32))
            xfb0_h = _stk.enter_context(nc.sbuf_tensor("xfb0", [P, 8 * 512], BF16))
            xfb1_h = _stk.enter_context(nc.sbuf_tensor("xfb1", [P, 8 * 512], BF16))
            wk_bf_h = _stk.enter_context(nc.sbuf_tensor("wk_bf", [P, 8 * C], BF16))
            wst0_h = _stk.enter_context(nc.sbuf_tensor("wst0", [P, C], F32))
            wst1_h = _stk.enter_context(nc.sbuf_tensor("wst1", [P, C], F32))
            wr_st_h = _stk.enter_context(nc.sbuf_tensor("wr_st", [P, 64], F32))
            z_sb_h = _stk.enter_context(nc.sbuf_tensor("z_sb", [P, NSLOT * E], F32))
            psA0_h = _stk.enter_context(nc.psum_tensor("psA0", [P, 1024], BF16))
            psA1_h = _stk.enter_context(nc.psum_tensor("psA1", [P, 1024], BF16))
            psK0_h = _stk.enter_context(nc.psum_tensor("psK0", [P, 512], F32))
            psK1_h = _stk.enter_context(nc.psum_tensor("psK1", [P, 512], F32))
            psZ_h = _stk.enter_context(nc.psum_tensor("psZ", [P, 512], F32))
            x_stage = x_stage_h.ap()
            xfb = [xfb0_h.ap(), xfb1_h.ap()]
            wk_bf = wk_bf_h.ap()
            wst = [wst0_h.ap(), wst1_h.ap()]
            wr_st = wr_st_h.ap()
            z_sb = z_sb_h.ap()
            psA = [psA0_h.ap(), psA1_h.ap()]
            psK = [psK0_h.ap(), psK1_h.ap()]
            psZ = psZ_h.ap()
            stat = {n: [_stk.enter_context(nc.sbuf_tensor(f"st_{n}{r}", [P, 1], F32)).ap()
                        for r in range(2)]
                    for n in ("sumx", "sumsq", "mean", "nmsq", "var", "lnv", "rstd",
                              "m1", "m2", "gs", "grs")}
            g8 = {n: [_stk.enter_context(nc.sbuf_tensor(f"g8_{n}{r}", [P, E], F32)).ap()
                      for r in range(2)]
                  for n in ("eq", "tmp", "zs", "ex", "ind", "es")}

            for half, dest in ((0, wq_bf), (1, wk_bf)):
                for cg in range(CG):
                    r = cg % 2
                    dma("sync", wst[r][:],
                        w_attn[cg * P:(cg + 1) * P, half * C:(half + 1) * C],
                        [f"wst{r}"], [f"wst{r}"])
                    cmp("gpsimd",
                        lambda e, r=r, d=dest, cg=cg: e.tensor_copy(
                            d[:, cg * C:(cg + 1) * C], wst[r][:]),
                        [f"wst{r}"], ["wqk"])
            dma("gpsimd", wr_st[:].rearrange("q (g f) -> q g f", g=CG),
                wr_in.rearrange("(g p) f -> p g f", p=P),
                [], ["wr_st"])
            cmp("gpsimd", lambda e: e.tensor_copy(wr_bf[:], wr_st[:]), ["wr_st"], ["wr"])

            for ib in range(8):
                dma("gpsimd", x_stage[:].rearrange("q (t c) -> q t c", t=4),
                    x_seq[ib * 512:(ib + 1) * 512, :].rearrange("(t p) c -> p t c", p=P),
                    ["xst"], ["xst"])
                for i in range(4):
                    p = ib * 4 + i
                    r = p % 2
                    xt = x_stage[:, i * C:(i + 1) * C]
                    xn_ap, xn_tok = xn_tile(p)
                    cmp("scalar", lambda e, xt=xt, r=r: e.activation(
                        out=wst[r][:], in_=xt, func=AF.Square,
                        accum_out=stat["sumsq"][r][:]),
                        ["xst", f"wst{r}"], [f"wst{r}", f"sumsq{r}"])
                    cmp("vector", lambda e, xt=xt, r=r: e.reduce_sum(
                        stat["sumx"][r][:], xt, axis=AX.X),
                        ["xst"], [f"sumx{r}"])
                    cmp("vector", lambda e, r=r: e.tensor_scalar_mul(
                        stat["mean"][r][:], stat["sumx"][r][:], 1.0 / C),
                        [f"sumx{r}"], [f"mean{r}"])
                    cmp("vector", lambda e, r=r: e.scalar_tensor_tensor(
                        stat["nmsq"][r][:], stat["mean"][r][:], -1.0, stat["mean"][r][:],
                        ALU.mult, ALU.mult),
                        [f"mean{r}"], [f"nmsq{r}"])
                    cmp("vector", lambda e, r=r: e.scalar_tensor_tensor(
                        stat["var"][r][:], stat["sumsq"][r][:], 1.0 / C, stat["nmsq"][r][:],
                        ALU.mult, ALU.add),
                        [f"sumsq{r}", f"nmsq{r}"], [f"var{r}"])
                    cmp("scalar", lambda e, r=r: e.activation(
                        out=stat["lnv"][r][:], in_=stat["var"][r][:], func=AF.Ln,
                        bias=eps_t[:]),
                        [f"var{r}", "eps"], [f"lnv{r}"])
                    cmp("scalar", lambda e, r=r: e.activation(
                        out=stat["rstd"][r][:], in_=stat["lnv"][r][:], func=AF.Exp,
                        scale=-0.5),
                        [f"lnv{r}"], [f"rstd{r}"])
                    cmp("vector", lambda e, xt=xt, r=r, xn_ap=xn_ap: e.tensor_scalar(
                        xn_ap, xt, stat["mean"][r][:], stat["rstd"][r][:],
                        ALU.subtract, ALU.mult),
                        ["xst", f"mean{r}", f"rstd{r}"], [xn_tok])
                br = ib % 2
                for g in range(CG):
                    pr = g % 2
                    for i in range(4):
                        p = ib * 4 + i
                        xn_ap, xn_tok = xn_tile(p)
                        cmp("tensor", lambda e, g=g, i=i, pr=pr, xn_ap=xn_ap: e.transpose(
                            psA[pr][:, i * P:(i + 1) * P],
                            xn_ap[:, g * P:(g + 1) * P], ident[:]),
                            [xn_tok, "ident", f"psA{pr}"], [f"psA{pr}"])
                    cmp("scalar", lambda e, g=g, pr=pr, br=br: e.activation(
                        out=xfb[br][:, g * 512:(g + 1) * 512],
                        in_=psA[pr][:, 0:512], func=AF.Copy),
                        [f"psA{pr}"], [f"xfb{br}"])
                for kg in range(CG):
                    kr = kg % 2
                    for cg in range(CG):
                        cmp("tensor", lambda e, kg=kg, cg=cg, kr=kr, br=br: e.matmul(
                            psK[kr][:],
                            wk_bf[:, cg * C + kg * P: cg * C + (kg + 1) * P],
                            xfb[br][:, cg * 512:(cg + 1) * 512],
                            start=(cg == 0), stop=(cg == CG - 1)),
                            [f"xfb{br}", "wqk", f"psK{kr}"], [f"psK{kr}"])
                    cmp("vector", lambda e, kg=kg, kr=kr, ib=ib: e.tensor_scalar(
                        k_fm[:, kg * 4096 + ib * 512: kg * 4096 + (ib + 1) * 512],
                        psK[kr][:], b_sb[:, 8 + kg:9 + kg], None, ALU.add),
                        [f"psK{kr}", "b_sb"], ["k_fm"])
                if ib < 4:
                    for tch in range(4):
                        p = ib * 4 + tch
                        for cg in range(CG):
                            cmp("tensor", lambda e, tch=tch, cg=cg, br=br: e.matmul(
                                psZ[:, 0:E],
                                xfb[br][:, cg * 512 + tch * P: cg * 512 + (tch + 1) * P],
                                wr_bf[:, cg * E:(cg + 1) * E],
                                start=(cg == 0), stop=(cg == CG - 1)),
                                [f"xfb{br}", "wr", "psZ"], ["psZ"])
                        cmp("scalar", lambda e, p=p: e.activation(
                            out=z_sb[:, p * E:(p + 1) * E], in_=psZ[:, 0:E], func=AF.Copy),
                            ["psZ"], ["z_sb"])

            for p in range(NSLOT):
                r = p % 2
                z = z_sb[:, p * E:(p + 1) * E]
                cmp("vector", lambda e, z=z, r=r: e.reduce_max(
                    stat["m1"][r][:], z, axis=AX.X), ["z_sb"], [f"gm1{r}"])
                cmp("vector", lambda e, z=z, r=r: e.tensor_scalar(
                    g8["eq"][r][:], z, stat["m1"][r][:], None, ALU.is_equal),
                    ["z_sb", f"gm1{r}"], [f"geq{r}"])
                cmp("vector", lambda e, z=z, r=r: e.scalar_tensor_tensor(
                    g8["tmp"][r][:], g8["eq"][r][:], NEG, z, ALU.mult, ALU.add),
                    [f"geq{r}", "z_sb"], [f"gtmp{r}"])
                cmp("vector", lambda e, r=r: e.reduce_max(
                    stat["m2"][r][:], g8["tmp"][r][:], axis=AX.X),
                    [f"gtmp{r}"], [f"gm2{r}"])
                cmp("vector", lambda e, z=z, r=r: e.tensor_scalar(
                    g8["zs"][r][:], z, stat["m1"][r][:], None, ALU.subtract),
                    ["z_sb", f"gm1{r}"], [f"gzs{r}"])
                cmp("scalar", lambda e, r=r: e.activation(
                    out=g8["ex"][r][:], in_=g8["zs"][r][:], func=AF.Exp),
                    [f"gzs{r}"], [f"gex{r}"])
                cmp("vector", lambda e, z=z, r=r: e.tensor_scalar(
                    g8["ind"][r][:], z, stat["m2"][r][:], None, ALU.is_ge),
                    ["z_sb", f"gm2{r}"], [f"gind{r}"])
                cmp("vector", lambda e, r=r: e.tensor_tensor(
                    g8["es"][r][:], g8["ex"][r][:], g8["ind"][r][:], ALU.mult),
                    [f"gex{r}", f"gind{r}"], [f"ges{r}"])
                cmp("vector", lambda e, r=r: e.reduce_sum(
                    stat["gs"][r][:], g8["es"][r][:], axis=AX.X),
                    [f"ges{r}"], [f"ggs{r}"])
                cmp("vector", lambda e, r=r: e.reciprocal(
                    stat["grs"][r][:], stat["gs"][r][:]),
                    [f"ggs{r}"], [f"ggrs{r}"])
                cmp("vector", lambda e, p=p, r=r: e.tensor_scalar(
                    w_gate[:, p * E:(p + 1) * E], g8["es"][r][:], stat["grs"][r][:],
                    None, ALU.mult),
                    [f"ges{r}", f"ggrs{r}"], ["w_gate"])
            tri_s = x_stage[:, 0:P]
            ones_f = x_stage[0:1, P:2 * P]
            ones_c = x_stage[:, 2 * P:2 * P + 1]
            erow = x_stage[0:1, 272:272 + E]
            brow = x_stage[0:1, 288:288 + E]
            iab_f = x_stage[:, 640:672]
            sel8 = [x_stage[:, 672 + 8 * r:680 + 8 * r] for r in range(2)]
            nse8 = [x_stage[:, 688 + 8 * r:696 + 8 * r] for r in range(2)]
            tmp8r = [x_stage[:, 704 + 8 * r:712 + 8 * r] for r in range(2)]
            rtoks = (["xst", "tris", "onesf", "onesc", "erow", "brow", "dest", "iab"]
                     + [f"{n}{r}" for n in ("sel8", "nse8", "tmp8r") for r in range(2)])
            cmp("gpsimd", lambda e: e.memset(x_stage[:, 0:C], 0.0), rtoks, rtoks)
            cmp("gpsimd", lambda e: e.memset(ones_f, 1.0), ["onesf"], ["onesf"])
            cmp("gpsimd", lambda e: e.memset(ones_c, 1.0), ["onesc"], ["onesc"])
            cmp("gpsimd", lambda e: e.memset(tri_s, 0.0), ["tris"], ["tris"])
            cmp("gpsimd", lambda e: e.affine_select(
                out=tri_s, in_=tri_s, compare_op=ALU.is_ge, fill=1.0,
                base=0, pattern=[[-1, P]], channel_multiplier=1), ["tris"], ["tris"])
            for _e in range(E):
                cmp("gpsimd", lambda e, _e=_e: e.memset(erow[0:1, _e:_e + 1], float(_e * CAP)),
                    ["erow"], ["erow"])
            cmp("vector", lambda e: e.tensor_copy(brow, erow),
                ["erow", "brow"], ["brow"])
            for p in range(NSLOT):
                r = p % 2
                wcol = w_gate[:, p * E:(p + 1) * E]
                cmp("vector", lambda e, wcol=wcol, r=r: e.tensor_scalar(
                    sel8[r], wcol, 0.0, None, ALU.is_gt),
                    ["w_gate", f"sel8{r}"], [f"sel8{r}"])
                cmp("vector", lambda e, wcol=wcol, r=r: e.tensor_scalar(
                    nse8[r], wcol, 0.0, None, ALU.is_le),
                    ["w_gate", f"nse8{r}"], [f"nse8{r}"])
                cmp("tensor", lambda e, r=r: e.matmul(
                    psK[r][:, 0:E], tri_s, sel8[r], start=True, stop=False),
                    ["tris", f"sel8{r}", f"psK{r}"], [f"psK{r}"])
                cmp("tensor", lambda e, r=r: e.matmul(
                    psK[r][:, 0:E], ones_f, brow, start=False, stop=True),
                    ["onesf", "brow", f"psK{r}"], [f"psK{r}"])
                cmp("vector", lambda e, p=p, r=r: e.scalar_tensor_tensor(
                    dest_f[:, p * E:(p + 1) * E], nse8[r], BIG, psK[r][:, 0:E],
                    ALU.mult, ALU.add),
                    [f"nse8{r}", f"psK{r}", "dest"], ["dest"])
                cmp("tensor", lambda e, r=r: e.matmul(
                    psZ[0:1, 0:E], ones_c, sel8[r], start=True, stop=True),
                    ["onesc", f"sel8{r}", "psZ"], ["psZ"])
                cmp("vector", lambda e: e.tensor_tensor(
                    brow, brow, psZ[0:1, 0:E], ALU.add),
                    ["psZ", "brow"], ["brow"])
                cmp("vector", lambda e, p=p: e.tensor_reduce(
                    iab_f[:, p:p + 1], dest_f[:, p * E:(p + 1) * E], AX.X, ALU.min),
                    ["dest", "iab"], ["iab"])
                cmp("vector", lambda e, p=p, r=r: e.tensor_scalar(
                    tmp8r[r], dest_f[:, p * E:(p + 1) * E], iab_f[:, p:p + 1],
                    None, ALU.is_equal),
                    ["dest", "iab", f"tmp8r{r}"], [f"tmp8r{r}"])
                cmp("vector", lambda e, p=p, r=r: e.scalar_tensor_tensor(
                    tmp8r[r], tmp8r[r], BIG, dest_f[:, p * E:(p + 1) * E],
                    ALU.mult, ALU.add),
                    [f"tmp8r{r}", "dest"], [f"tmp8r{r}"])
                cmp("vector", lambda e, p=p, r=r: e.tensor_reduce(
                    iab_f[:, 16 + p:17 + p], tmp8r[r], AX.X, ALU.min),
                    [f"tmp8r{r}", "iab"], ["iab"])
            cmp("vector", lambda e: e.tensor_copy(iab_i[:], iab_f),
                ["iab", "iabi"], ["iabi"])
            if dbg:
                dma("gpsimd", d_zsb, z_sb[:], ["z_sb"], ["d1"])
                dma("gpsimd", d_wgate, w_gate[:], ["w_gate"], ["d2"])

        em.fence()
        with contextlib.ExitStack() as _stk:
            hw_all_h = _stk.enter_context(nc.sbuf_tensor("hw_all", [P, HEADS * C], BF16))

            probs_h = _stk.enter_context(nc.sbuf_tensor("probs", [P, 4096], BF16))
            probsB_h = _stk.enter_context(nc.sbuf_tensor("probsB", [P, 4096], BF16))
            pT0_h = _stk.enter_context(nc.sbuf_tensor("probsT0", [P, 512], BF16))
            pT1_h = _stk.enter_context(nc.sbuf_tensor("probsT1", [P, 512], BF16))
            pT2_h = _stk.enter_context(nc.sbuf_tensor("probsT2", [P, 512], BF16))
            y_slot_h = _stk.enter_context(nc.sbuf_tensor("y_slot", [P, C], F32))
            y_hb_h = _stk.enter_context(nc.sbuf_tensor("y_hb", [P, C], BF16))
            q0_h = _stk.enter_context(nc.sbuf_tensor("q_slot0", [P, C], BF16))
            q1_h = _stk.enter_context(nc.sbuf_tensor("q_slot1", [P, C], BF16))

            nmblk_h = _stk.enter_context(nc.sbuf_tensor("nmblk", [P, 16], F32))
            ebb_h = _stk.enter_context(nc.sbuf_tensor("ebb", [P, 16], F32))
            nM_h = _stk.enter_context(nc.sbuf_tensor("nM", [P, 2], F32))
            scv_h = _stk.enter_context(nc.sbuf_tensor("scv", [P, 16], F32))

            den_h = _stk.enter_context(nc.sbuf_tensor("den", [P, 2], F32))
            recip_h = _stk.enter_context(nc.sbuf_tensor("recip", [P, 2], F32))
            psS0_h = _stk.enter_context(nc.psum_tensor("psS0", [P, 512], F32))
            psS1_h = _stk.enter_context(nc.psum_tensor("psS1", [P, 512], F32))
            psS2_h = _stk.enter_context(nc.psum_tensor("psS2", [P, 512], F32))
            psT0_h = _stk.enter_context(nc.psum_tensor("psT0", [P, 512], BF16))
            psT1_h = _stk.enter_context(nc.psum_tensor("psT1", [P, 512], BF16))
            psT2_h = _stk.enter_context(nc.psum_tensor("psT2", [P, 512], BF16))
            psY0_h = _stk.enter_context(nc.psum_tensor("psY0", [P, 512], F32))
            psY1_h = _stk.enter_context(nc.psum_tensor("psY1", [P, 512], F32))

            hw_all = hw_all_h.ap()

            probs2 = [probs_h.ap(), probsB_h.ap()]
            probsT = [pT0_h.ap(), pT1_h.ap(), pT2_h.ap()]
            y_slot = y_slot_h.ap()
            y_hb2 = [y_hb_h.ap(), y_hb_h.ap()]
            q_slot = [q0_h.ap(), q1_h.ap()]
            _xfs = y_slot[:].bitcast(BF16)[:, 0:C]
            xfs = [_xfs, _xfs]
            nmblk = nmblk_h.ap()
            ebb = ebb_h.ap()
            nM = nM_h.ap()
            scv = scv_h.ap()
            sci = ebb
            den = den_h.ap()
            recip = recip_h.ap()
            psS = [psS0_h.ap(), psS1_h.ap(), psS2_h.ap()]
            psT = [psT0_h.ap(), psT1_h.ap(), psT2_h.ap()]
            psY = [psY0_h.ap(), psY1_h.ap()]
            psQ = psS[0]

            s_ctr = [0]

            def next_s():
                s_ctr[0] += 1
                return s_ctr[0] % 3

            hw_st = y_slot[0:1, 0:C]
            for h in range(HEADS):
                dma("gpsimd", hw_st, head_w[h:h + 1, :], ["hwst"], ["hwst", "y_slot"])
                for half in range(2):
                    sr = next_s()
                    cmp("tensor", lambda e, half=half, sr=sr: e.matmul(
                        psS[sr][:], ones_row[:], hw_st[:, half * 512:(half + 1) * 512],
                        start=True, stop=True),
                        ["hwst", "ones", "y_slot", f"psS{sr}"], [f"psS{sr}"])
                    cmp("scalar", lambda e, h=h, half=half, sr=sr: e.activation(
                        out=hw_all[:, h * C + half * 512: h * C + (half + 1) * 512],
                        in_=psS[sr][:], func=AF.Copy),
                        [f"psS{sr}"], ["hw_all"])

            def s_tiles(j):
                return list(range(0, j + 1)) + list(range(16, 16 + j + 1))

            def qk_blocks(j):
                blocks = []
                for run_start, run_len, phys0 in ((0, j + 1, 0), (j + 1, j + 1, 16)):
                    off = 0
                    while off < run_len:
                        n = min(4, run_len - off)
                        blocks.append((run_start + off, n, phys0 + off))
                        off += n
                return blocks

            def qk_matmul(j, h, pcol, n, kphys, sr, sp):
                qg, qrow = h // 2, (h % 2) * HS
                cmp("tensor", lambda e: e.matmul(
                    psS[sr][:, 0:n * P],
                    q_slot[sp][qrow:qrow + HS, qg * P:(qg + 1) * P],
                    k_fm[qrow:qrow + HS,
                         (h // 2) * 4096 + kphys * P: (h // 2) * 4096 + (kphys + n) * P],
                    start=True, stop=True),
                    [f"qs{sp}", "k_fm", f"psS{sr}"], [f"psS{sr}"])

            def s2a(j, h):
                hp = h % 2
                sp = j % 2
                probs = probs2[hp]
                ptok = f"probs{hp}"
                co = hp * 8
                blocks = qk_blocks(j)
                for bi, (pcol, n, kphys) in enumerate(blocks):
                    sr = next_s()
                    qk_matmul(j, h, pcol, n, kphys, sr, sp)
                    cmp("vector", lambda e, n=n, sr=sr, co=co, bi=bi: e.reduce_max(
                        nmblk[:, co + bi:co + bi + 1], psS[sr][:, 0:n * P], axis=AX.X,
                        negate=True),
                        [f"psS{sr}"], [f"nmblk{hp}"])
                    cmp("vector", lambda e, co=co, bi=bi: e.tensor_scalar_mul(
                        ebb[:, co + bi:co + bi + 1], nmblk[:, co + bi:co + bi + 1], 0.125),
                        [f"nmblk{hp}", f"ebb{hp}"], [f"ebb{hp}"])
                    cmp("scalar", lambda e, probs=probs, pcol=pcol, n=n, sr=sr,
                        co=co, bi=bi: e.activation(
                        out=probs[:, pcol * P:(pcol + n) * P], in_=psS[sr][:, 0:n * P],
                        func=AF.Exp, bias=ebb[:, co + bi:co + bi + 1], scale=0.125),
                        [f"psS{sr}", f"ebb{hp}", ptok], [ptok])
                nb = len(blocks)
                cmp("vector", lambda e, co=co, hp=hp: e.tensor_copy(
                    nM[:, hp:hp + 1], nmblk[:, co:co + 1]),
                    [f"nmblk{hp}", f"nM{hp}"], [f"nM{hp}"])
                for bi in range(1, nb):
                    cmp("vector", lambda e, co=co, bi=bi, hp=hp: e.tensor_tensor(
                        nM[:, hp:hp + 1], nM[:, hp:hp + 1], nmblk[:, co + bi:co + bi + 1],
                        ALU.min),
                        [f"nmblk{hp}", f"nM{hp}"], [f"nM{hp}"])
                cmp("vector", lambda e, co=co, nb=nb, hp=hp: e.tensor_tensor(
                    sci[:, co:co + nb], nM[:, hp:hp + 1].broadcast_to((P, nb)),
                    nmblk[:, co:co + nb], ALU.subtract),
                    [f"nM{hp}", f"nmblk{hp}", f"ebb{hp}"], [f"ebb{hp}"])
                cmp("scalar", lambda e, co=co, nb=nb: e.activation(
                    out=scv[:, co:co + nb], in_=sci[:, co:co + nb],
                    func=AF.Exp, scale=0.125),
                    [f"ebb{hp}", f"scv{hp}"], [f"scv{hp}"])
                for bi, (pcol, n, kphys) in enumerate(blocks):
                    cmp("vector", lambda e, probs=probs, pcol=pcol, n=n, co=co, bi=bi:
                        e.tensor_scalar(
                            probs[:, pcol * P:(pcol + n) * P],
                            probs[:, pcol * P:(pcol + n) * P],
                            scv[:, co + bi:co + bi + 1], None, ALU.mult),
                        [f"scv{hp}", ptok], [ptok])
                cmp("vector", lambda e, probs=probs, j=j: e.tensor_tensor(
                    probs[:, j * P:(j + 1) * P], probs[:, j * P:(j + 1) * P],
                    tri_bf[:], ALU.mult),
                    [ptok, "tri"], [ptok])
                cmp("vector", lambda e, probs=probs, j=j: e.tensor_scalar(
                    probs[:, (2 * j + 1) * P:(2 * j + 2) * P],
                    probs[:, (2 * j + 1) * P:(2 * j + 2) * P],
                    padv_sb[:], None, ALU.mult),
                    [ptok, "padv"], [ptok])
                nst = 2 * (j + 1)
                cmp("vector", lambda e, probs=probs, nst=nst, hp=hp: e.reduce_sum(
                    den[:, hp:hp + 1], probs[:, 0:nst * P], axis=AX.X),
                    [ptok], [f"den{hp}"])
                cmp("vector", lambda e, hp=hp: e.reciprocal(
                    recip[:, hp:hp + 1], den[:, hp:hp + 1]),
                    [f"den{hp}"], [f"recip{hp}"])

            def s2b(j, h, part):
                hp = h % 2
                probs = probs2[hp]
                ptok = f"probs{hp}"
                tiles = s_tiles(j)
                nst = len(tiles)
                groups = list(range(0, nst, 4))

                def emit_T(g0):
                    gn = min(4, nst - g0)
                    tr = (g0 // 4) % 3
                    for i in range(gn):
                        st = g0 + i
                        cmp("tensor", lambda e, probs=probs, st=st, i=i, tr=tr: e.transpose(
                            psT[tr][:, i * P:(i + 1) * P],
                            probs[:, st * P:(st + 1) * P], ident[:]),
                            [ptok, "ident", f"psT{tr}"], [f"psT{tr}"])
                    cmp("scalar", lambda e, gn=gn, tr=tr: e.activation(
                        out=probsT[tr][:, 0:gn * P], in_=psT[tr][:, 0:gn * P],
                        func=AF.Copy),
                        [f"psT{tr}"], [f"pT{tr}"])

                def emit_AV(g0):
                    gn = min(4, nst - g0)
                    tr = (g0 // 4) % 3
                    for i in range(gn):
                        st = g0 + i
                        phys = tiles[st]
                        xn_ap, xn_tok = xn_tile(phys)
                        first, last = (st == 0), (st == nst - 1)
                        for half in range(2):
                            cmp("tensor",
                                lambda e, i=i, tr=tr, half=half, xn_ap=xn_ap,
                                first=first, last=last: e.matmul(
                                    psY[half][:], probsT[tr][:, i * P:(i + 1) * P],
                                    xn_ap[:, half * 512:(half + 1) * 512],
                                    start=first, stop=last),
                                [f"pT{tr}", xn_tok, f"psY{half}"], [f"psY{half}"])

                if part == 0:
                    for g0 in groups:
                        emit_T(g0)
                    return
                for g0 in groups:
                    emit_AV(g0)
                y_hb = y_hb2[hp]
                ytok = "y_h"
                for half in range(2):
                    cmp("scalar", lambda e, y_hb=y_hb, half=half, hp=hp: e.activation(
                        out=y_hb[:, half * 512:(half + 1) * 512], in_=psY[half][:],
                        func=AF.Copy, scale=recip[:, hp:hp + 1]),
                        [f"psY{half}", f"recip{hp}", ytok], [ytok])
                cmp("vector", lambda e, y_hb=y_hb, h=h: e.tensor_tensor(
                    y_hb[:], y_hb[:], hw_all[:, h * C:(h + 1) * C], ALU.mult),
                    [ytok, "hw_all"], [ytok])
                if h == 0:
                    cmp("vector", lambda e, y_hb=y_hb: e.tensor_copy(y_slot[:], y_hb[:]),
                        [ytok, "y_slot"], ["y_slot"])
                else:
                    cmp("vector", lambda e, y_hb=y_hb: e.tensor_tensor(
                        y_slot[:], y_slot[:], y_hb[:], ALU.add),
                        [ytok, "y_slot"], ["y_slot"])

            for j in range(NSLOT):
                sp = j % 2
                for g in range(CG):
                    tr = g % 2
                    cmp("tensor", lambda e, g=g, j=j, tr=tr: e.transpose(
                        psT[tr][:, (g % 4) * P:((g % 4) + 1) * P],
                        xn_own[:, j * C + g * P: j * C + (g + 1) * P], ident[:]),
                        [f"xn{j}", "ident", f"psT{tr}"], [f"psT{tr}"])
                    cmp("scalar", lambda e, g=g, tr=tr, sp=sp: e.activation(
                        out=xfs[sp][:, g * P:(g + 1) * P],
                        in_=psT[tr][:, (g % 4) * P:((g % 4) + 1) * P], func=AF.Copy),
                        [f"psT{tr}", f"xfs{sp}", "y_slot"], [f"xfs{sp}", "y_slot"])
                for rnd in range(2):
                    for qi in range(4):
                        qg = rnd * 4 + qi
                        for cg in range(CG):
                            cmp("tensor", lambda e, qg=qg, qi=qi, cg=cg, sp=sp: e.matmul(
                                psQ[:, qi * P:(qi + 1) * P],
                                wq_bf[:, cg * C + qg * P: cg * C + (qg + 1) * P],
                                xfs[sp][:, cg * P:(cg + 1) * P],
                                start=(cg == 0), stop=(cg == CG - 1)),
                                [f"xfs{sp}", "y_slot", "wqk", "psS0"], ["psS0"])
                    for qi in range(4):
                        qg = rnd * 4 + qi
                        cmp("vector", lambda e, qg=qg, qi=qi, sp=sp: e.tensor_scalar(
                            q_slot[sp][:, qg * P:(qg + 1) * P],
                            psQ[:, qi * P:(qi + 1) * P], b_sb[:, qg:qg + 1], None, ALU.add),
                            ["psS0", "b_sb", f"qs{sp}"], [f"qs{sp}"])
                for t in range(HEADS + 1):
                    if t >= 1:
                        s2b(j, t - 1, 0)
                    if t < HEADS:
                        s2a(j, t)
                    if t >= 1:
                        s2b(j, t - 1, 1)
                dma("gpsimd", y_attn[j * P:(j + 1) * P, :], y_slot[:],
                    ["y_slot"], [f"ya{j}"])
            if dbg:
                dma("gpsimd", d_xnown, xn_own[:], [f"xn{p}" for p in range(16)], ["d3"])
                dma("gpsimd", d_xnoth, xn_oth[:], [f"xn{p}" for p in range(16, 32)], ["d4"])
                dma("gpsimd", d_kfm, k_fm[:], ["k_fm"], ["d5"])
                dma("gpsimd", d_probs, probs2[1][:], ["probs1"], ["d6"])
                dma("gpsimd", d_hwall, hw_all[:], ["hw_all"], ["d7"])
                dma("gpsimd", d_qslot, q_slot[1][:], ["qs1"], ["d8"])
                cmp("gpsimd", lambda e: e.tensor_copy(
                    d_small[:, 0:8] if False else d_small[:, 0:8], d_small[:, 0:8])
                    if False else e.engine_nop(), [], [])
                dma("gpsimd", d_small[:, 0:2], nrun[:], ["nrun0", "nrun1"], ["d9"])
                dma("gpsimd", d_small[:, 2:4], den[:], ["den0", "den1"], ["d10"])
                dma("gpsimd", d_small[:, 4:6], recip[:], ["recip0", "recip1"], ["d11"])
                dma("gpsimd", d_small[:, 6:8], ebias[:], ["ebias0", "ebias1"], ["d12"])

    em.fence()
    with contextlib.ExitStack() as _stk:
        y_acc_h = _stk.enter_context(nc.sbuf_tensor("y_acc", [P, NSLOT * C], F32))
        yp_sb_h = _stk.enter_context(nc.sbuf_tensor("yp_sb", [P, NTP * C], BF16))
        xp_fm_h = _stk.enter_context(nc.sbuf_tensor("xp_fm", [P, CG * CAP], BF16))
        xp_fmB_h = _stk.enter_context(nc.sbuf_tensor("xp_fmB", [P, CG * CAP], BF16))
        w1_bf_h = _stk.enter_context(nc.sbuf_tensor("w1_bf", [P, CG * C], BF16))
        w2_bf_h = _stk.enter_context(nc.sbuf_tensor("w2_bf", [P, CG * C], BF16))
        wstc0_h = _stk.enter_context(nc.sbuf_tensor("wstc0", [P, 2 * C], F32))
        wstc1_h = _stk.enter_context(nc.sbuf_tensor("wstc1", [P, 2 * C], F32))
        wstc2_h = _stk.enter_context(nc.sbuf_tensor("wstc2", [P, 2 * C], F32))
        hid0_h = _stk.enter_context(nc.sbuf_tensor("hid0", [P, 4096], BF16))
        hid1_h = _stk.enter_context(nc.sbuf_tensor("hid1", [P, 2048], BF16))
        xpt0_h = _stk.enter_context(nc.sbuf_tensor("xpt0", [P, C], BF16))
        xpt1_h = _stk.enter_context(nc.sbuf_tensor("xpt1", [P, C], BF16))
        prm0_h = _stk.enter_context(nc.sbuf_tensor("prm0", [P, P], BF16))
        prm1_h = _stk.enter_context(nc.sbuf_tensor("prm1", [P, P], BF16))
        dcl_h = _stk.enter_context(nc.sbuf_tensor("dcl", [P, 2], F32))
        iota_t_h = _stk.enter_context(nc.sbuf_tensor("iota_t", [P, P], BF16))

        psM0_h = _stk.enter_context(nc.psum_tensor("psM0", [P, 512], F32))
        psM1_h = _stk.enter_context(nc.psum_tensor("psM1", [P, 512], F32))
        psW0a_h = _stk.enter_context(nc.psum_tensor("psW0a", [P, 512], F32))
        psW0b_h = _stk.enter_context(nc.psum_tensor("psW0b", [P, 512], F32))
        psW1a_h = _stk.enter_context(nc.psum_tensor("psW1a", [P, 512], F32))
        psW1b_h = _stk.enter_context(nc.psum_tensor("psW1b", [P, 512], F32))
        psX0_h = _stk.enter_context(nc.psum_tensor("psX0", [P, 1024], BF16))
        psX1_h = _stk.enter_context(nc.psum_tensor("psX1", [P, 1024], BF16))
        y_acc = y_acc_h.ap()
        yp_sb = yp_sb_h.ap()
        xp_fm2 = [xp_fm_h.ap(), xp_fmB_h.ap()]
        w1_bf2 = [w1_bf_h.ap(), w1_bf_h.ap()]
        w2_bf2 = [w2_bf_h.ap(), w2_bf_h.ap()]
        wstc = [wstc0_h.ap(), wstc1_h.ap(), wstc2_h.ap()]
        hid = [hid0_h.ap(), hid1_h.ap()]
        xpt = [xpt0_h.ap(), xpt1_h.ap()]
        prm = [prm0_h.ap(), prm1_h.ap()]
        dcl = dcl_h.ap()
        iota_t = iota_t_h.ap()
        ya_st = [hid[0].bitcast(F32)[:, 0:C], hid[1].bitcast(F32)[:, 0:C]]
        psM = [psM0_h.ap(), psM1_h.ap()]
        psW = [[psW0a_h.ap(), psW0b_h.ap()], [psW1a_h.ap(), psW1b_h.ap()]]
        psX = [psX0_h.ap(), psX1_h.ap()]
        YPALL = [f"yp{e_}" for e_ in range(E)]

        dma("gpsimd", iota_t[:], iota01, [], ["iota"])
        for p in range(NSLOT):
            r = p % 2
            dma("gpsimd", y_acc[:, p * C:(p + 1) * C], x_seq[p * P:(p + 1) * P, :],
                [f"yac{p}"], [f"yac{p}"])
            dma("gpsimd", ya_st[r], y_attn[p * P:(p + 1) * P, :],
                [f"ya{p}", f"yast{r}", f"hid{r}"], [f"yast{r}", f"hid{r}"])
            cmp("vector", lambda e, p=p, r=r: e.tensor_tensor(
                y_acc[:, p * C:(p + 1) * C], y_acc[:, p * C:(p + 1) * C],
                ya_st[r], ALU.add),
                [f"yast{r}", f"hid{r}", f"yac{p}"], [f"yac{p}"])
        if dbg:
            dma("gpsimd", d_yinit, y_acc[:], [f"yac{p}" for p in range(NSLOT)], ["d18"])

        def xp_prep(e_, t6_list=None):
            ep = e_ % 2
            xp_fm = xp_fm2[ep]
            for t6 in (range(NTP) if t6_list is None else t6_list):
                r = t6 % 2
                for p in range(NSLOT):
                    pr = p % 2
                    cmp("vector", lambda e, p=p, pr=pr, e_=e_, t6=t6: e.tensor_scalar(
                        dcl[:, pr:pr + 1], dest_f[:, p * E + e_: p * E + e_ + 1],
                        -float(e_ * CAP + t6 * P), None, ALU.add),
                        ["dest", f"dcl{pr}"], [f"dcl{pr}"])
                    cmp("vector", lambda e, p=p, pr=pr, e_=e_: e.tensor_scalar(
                        prm[pr][:], iota_t[:], dcl[:, pr:pr + 1],
                        w_gate[:, p * E + e_: p * E + e_ + 1],
                        ALU.is_equal, ALU.mult),
                        ["iota", f"dcl{pr}", "w_gate", f"prm{pr}"], [f"prm{pr}"])
                    for half in range(2):
                        cmp("tensor", lambda e, p=p, pr=pr, half=half: e.matmul(
                            psM[half][:],
                            prm[pr][:],
                            xn_own[:, p * C + half * 512: p * C + (half + 1) * 512],
                            start=(p == 0), stop=(p == NSLOT - 1)),
                            [f"prm{pr}", f"xn{p}", f"psM{half}"], [f"psM{half}"])
                for half in range(2):
                    cmp("scalar", lambda e, r=r, half=half: e.activation(
                        out=xpt[r][:, half * 512:(half + 1) * 512], in_=psM[half][:],
                        func=AF.Copy),
                        [f"psM{half}", f"xpt{r}"], [f"xpt{r}"])
                for g in range(CG):
                    cmp("tensor", lambda e, g=g, r=r: e.transpose(
                        psX[r][:, g * P:(g + 1) * P],
                        xpt[r][:, g * P:(g + 1) * P], ident[:]),
                        [f"xpt{r}", "ident", f"psX{r}"], [f"psX{r}"])
                cmp("scalar", lambda e, xp_fm=xp_fm, t6=t6, r=r: e.activation(
                    out=xp_fm[:].rearrange("q (g t) -> q g t", g=CG)[:, :, t6 * P:(t6 + 1) * P],
                    in_=psX[r][:].rearrange("q (g t) -> q g t", g=CG),
                    func=AF.Copy),
                    [f"psX{r}", f"xpfm{ep}"], [f"xpfm{ep}"])

        def load_w(idx, e_, fc):
            wp = idx % 2
            w1_bf = w1_bf2[wp]
            w2_bf = w2_bf2[wp]
            for cg2 in range(CG // 2):
                r = cg2 % 3
                dma("sync", wstc[r][:].rearrange("q (g c) -> q g c", g=2),
                    w1_in[e_, cg2 * 2 * P:(cg2 * 2 + 2) * P, fc * C:(fc + 1) * C]
                    .rearrange("(g p) c -> p g c", p=P),
                    [f"wstc{r}"], [f"wstc{r}"])
                cmp("scalar", lambda e, w1_bf=w1_bf, r=r, cg2=cg2: e.activation(
                    out=w1_bf[:, cg2 * 2 * C:(cg2 * 2 + 2) * C], in_=wstc[r][:],
                    func=AF.Copy),
                    [f"wstc{r}", "w1bf"], ["w1bf"])
            for fs2 in range(CG // 2):
                r = (fs2 + 1) % 3
                dma("gpsimd", wstc[r][:].rearrange("q (g c) -> q g c", g=2),
                    w2_in[e_, fc * C + fs2 * 2 * P: fc * C + (fs2 * 2 + 2) * P, :]
                    .rearrange("(g p) c -> p g c", p=P),
                    [f"wstc{r}"], [f"wstc{r}"])
                cmp("scalar", lambda e, w2_bf=w2_bf, r=r, fs2=fs2: e.activation(
                    out=w2_bf[:, fs2 * 2 * C:(fs2 * 2 + 2) * C], in_=wstc[r][:],
                    func=AF.Copy),
                    [f"wstc{r}", "w2bf"], ["w2bf"])

        TBS = [(0, 512), (512, 256)]
        pairs = [(e_, fc) for e_ in range(E) for fc in range(4)]
        load_w(0, *pairs[0])
        xp_prep(0)
        for idx, (e_, fc) in enumerate(pairs):
            wp = idx % 2
            ep = e_ % 2
            w1_bf = w1_bf2[wp]
            w2_bf = w2_bf2[wp]
            xp_fm = xp_fm2[ep]
            for tbi, (tb_off, tbsz) in enumerate(TBS):
                hr = tbi % 2
                for fs in range(CG):
                    mr = fs % 2
                    for cg in range(CG):
                        cmp("tensor", lambda e, w1_bf=w1_bf, xp_fm=xp_fm, fs=fs, cg=cg,
                            mr=mr, tb_off=tb_off, tbsz=tbsz: e.matmul(
                                psM[mr][:, 0:tbsz],
                                w1_bf[:, cg * C + fs * P: cg * C + (fs + 1) * P],
                                xp_fm[:, cg * CAP + tb_off: cg * CAP + tb_off + tbsz],
                                start=(cg == 0), stop=(cg == CG - 1)),
                            [f"xpfm{ep}", "w1bf", f"psM{mr}"], [f"psM{mr}"])
                    cmp("scalar", lambda e, fs=fs, mr=mr, hr=hr, tbsz=tbsz: e.activation(
                        out=hid[hr][:, fs * tbsz: (fs + 1) * tbsz],
                        in_=psM[mr][:, 0:tbsz], func=AF.Relu),
                        [f"psM{mr}", f"hid{hr}"], [f"hid{hr}"])
                for i in range(tbsz // P):
                    gi = tb_off // P + i
                    wr_ = gi % 2
                    for fs in range(CG):
                        for half in range(2):
                            cmp("tensor",
                                lambda e, w2_bf=w2_bf, fs=fs, i=i, half=half, wr_=wr_,
                                hr=hr, tbsz=tbsz: e.matmul(
                                    psW[wr_][half][:],
                                    hid[hr][:, fs * tbsz + i * P: fs * tbsz + (i + 1) * P],
                                    w2_bf[:, fs * C + half * 512: fs * C + (half + 1) * 512],
                                    start=(fs == 0), stop=(fs == CG - 1)),
                                [f"hid{hr}", "w2bf", f"psW{wr_}{half}"],
                                [f"psW{wr_}{half}"])
                    for half in range(2):
                        dst = yp_sb[:, gi * C + half * 512: gi * C + (half + 1) * 512]
                        if fc == 0:
                            cmp("vector", lambda e, dst=dst, half=half, wr_=wr_:
                                e.tensor_copy(dst, psW[wr_][half][:]),
                                [f"psW{wr_}{half}", "ypsb"], ["ypsb"])
                        else:
                            cmp("vector", lambda e, dst=dst, half=half, wr_=wr_:
                                e.tensor_tensor(dst, dst, psW[wr_][half][:], ALU.add),
                                [f"psW{wr_}{half}", "ypsb"], ["ypsb"])
            if fc == 3:
                for t6 in range(NTP):
                    dma("gpsimd", yp[e_ * CAP + t6 * P: e_ * CAP + (t6 + 1) * P, :],
                        yp_sb[:, t6 * C:(t6 + 1) * C],
                        ["ypsb"], [f"yp{e_}"])
            if idx + 1 < len(pairs):
                load_w(idx + 1, *pairs[idx + 1])
            if e_ + 1 < E:
                xp_prep(e_ + 1, [[0, 1], [2, 3], [4], [5]][fc])

        for p in range(NSLOT):
            for k in range(2):
                r = (2 * p + k) % 2
                col = p if k == 0 else 16 + p
                em.emit("gpsimd", lambda e, r=r, col=col: e.indirect_dma_start(
                    xpt[r][:], None,
                    yp[0:E * CAP, :],
                    bass.IndirectOffsetOnAxis(iab_i[:, col:col + 1], 0)),
                    reads=YPALL + ["iabi", f"xpt{r}"], writes=[f"xpt{r}"],
                    kind="dma")
                cmp("vector", lambda e, p=p, r=r: e.tensor_tensor(
                    y_acc[:, p * C:(p + 1) * C], y_acc[:, p * C:(p + 1) * C],
                    xpt[r][:], ALU.add),
                    [f"xpt{r}", f"yac{p}"], [f"yac{p}"])

        for p in range(NSLOT):
            dma("gpsimd", y_out[p * P:(p + 1) * P, :], y_acc[:, p * C:(p + 1) * C],
                [f"yac{p}"], [f"out{p}"])
        cmp("gpsimd", lambda e: e.engine_nop(),
            [f"out{p}" for p in range(NSLOT)]
            + ([f"d{i}" for i in range(1, 13)] + ["d18"]
               + [f"ya{j}" for j in range(NSLOT)] if dbg else []), [])

    em.finalize()

    with _sem_stack(nc, em) as sems:
        with nc.Block() as block:
            @block.sync
            def _(eng):
                em.replay("sync", eng, sems)

            @block.scalar
            def _(eng):
                em.replay("scalar", eng, sems)

            @block.vector
            def _(eng):
                em.replay("vector", eng, sems)

            @block.tensor
            def _(eng):
                em.replay("tensor", eng, sems)

            @block.gpsimd
            def _(eng):
                em.replay("gpsimd", eng, sems)

    return nc


_CACHED = {}


def kernel(L0_attn_x, L0_attn_W_attn, L0_attn_b_attn, L0_attn_head_weights,
           L0_ffn_Wr, L0_ffn_W1, L0_ffn_W2):
    x = np.ascontiguousarray(np.asarray(L0_attn_x, dtype=np.float32))
    W_attn = np.ascontiguousarray(np.asarray(L0_attn_W_attn, dtype=np.float32))
    b_att = np.ascontiguousarray(np.asarray(L0_attn_b_attn, dtype=np.float32))
    head_w = np.ascontiguousarray(np.asarray(L0_attn_head_weights, dtype=np.float32))
    Wr = np.ascontiguousarray(np.asarray(L0_ffn_Wr, dtype=np.float32))
    W1 = np.ascontiguousarray(np.asarray(L0_ffn_W1, dtype=np.float32))
    W2 = np.ascontiguousarray(np.asarray(L0_ffn_W2, dtype=np.float32))

    import ml_dtypes
    tri = np.where(np.arange(P)[:, None] >= np.arange(P)[None, :],
                   1.0, 0.0).astype(ml_dtypes.bfloat16)
    iota = np.tile(np.arange(P, dtype=np.float32), (P, 1)).astype(ml_dtypes.bfloat16)

    if "nc" not in _CACHED:
        _CACHED["nc"] = build_program()
    nc = _CACHED["nc"]

    in_maps = []
    for c in range(8):
        b, r = c % 4, c // 4
        own = [2 * p + r for p in range(NSLOT)]
        oth = [2 * p + (1 - r) for p in range(NSLOT)]
        rows = np.concatenate([np.arange(t * P, (t + 1) * P) for t in own + oth])
        x_seq_c = np.ascontiguousarray(x[b][rows])
        padval = np.full((P, 1), 0.0 if r == 0 else 1.0, dtype=np.float32)
        in_maps.append(dict(
            x_seq=x_seq_c, w_attn=W_attn, b_attn=b_att, head_w=head_w,
            wr_in=Wr, w1_in=W1, w2_in=W2, tri01=tri, iota01=iota, padv=padval,
        ))

    res = run_bass_kernel_spmd(nc, in_maps, core_ids=list(range(8)))

    out = np.empty((4, 4096, C), dtype=np.float32)
    for c in range(8):
        b, r = c % 4, c // 4
        yo = np.asarray(res.results[c]["y_out"], dtype=np.float32)
        for p in range(NSLOT):
            t = 2 * p + r
            out[b, t * P:(t + 1) * P, :] = yo[p * P:(p + 1) * P, :]
    return out
